# version 03 (final)
# baseline (speedup 1.0000x reference)
import numpy as np
from contextlib import ExitStack

import concourse.bass as bass
import concourse.mybir as mybir
from concourse.bass_utils import run_bass_kernel_spmd

F32 = mybir.dt.float32
F32R = mybir.dt.float32r
I32 = mybir.dt.int32
AF = mybir.ActivationFunctionType
ALU = mybir.AluOpType
AX = mybir.AxisListType

NCORE = 8
D = 1024
SEQ = 2048
BPC = 2
T = BPC * SEQ
DB = D // 128
NT = T // 512

ENGS = ("sync", "tensor", "vector", "scalar", "gpsimd")


class Sem:
    __slots__ = ("h", "n", "name", "uid")

    def __init__(self, h, name, uid):
        self.h, self.n, self.name, self.uid = h, 0, name, uid


class KB:
    def __init__(self, nc, stack):
        self.nc = nc
        self.stack = stack
        self.streams = {e: [] for e in ENGS}
        self.waited = {e: {} for e in ENGS}
        self.nsem = 0
        self.strict_sems = []
        self.all_sems = []
        self.maxwait = {}

    def sem(self, name=None):
        self.nsem += 1
        nm = name or f"s{self.nsem}"
        h = self.stack.enter_context(self.nc.semaphore(nm))
        s = Sem(h, nm, self.nsem)
        self.all_sems.append(s)
        return s

    def emit(self, eng, build, waits=(), incs=()):
        if self.strict_sems:
            waits = list(waits) + [(s, s.n) for s in self.strict_sems]
        w = []
        for s, v in waits:
            if v > 0 and self.waited[eng].get(s.uid, -1) < v:
                w.append((s.h, v))
                self.waited[eng][s.uid] = v
                self.maxwait.setdefault(s.uid, (s, 0))
                if v > self.maxwait[s.uid][1]:
                    self.maxwait[s.uid] = (s, v)
        inc2 = []
        for s, k in incs:
            s.n += k
            inc2.append((s.h, k))

        def fn(e, w=w, inc2=inc2, build=build):
            for h, v in w:
                e.wait_ge(h, v)
            ins = build(e)
            for h, k in inc2:
                ins = ins.then_inc(h, k)

        self.streams[eng].append(fn)

    def dma(self, eng, out, in_, waits=(), sem=None):
        incs = ((sem, 16),) if sem is not None else ()
        self.emit(eng, lambda e, o=out, i=in_: e.dma_start(out=o, in_=i), waits, incs)

    def mm(self, ps, lhsT, rhs, start, stop, waits=(), incs=()):
        self.emit("tensor",
                  lambda e, o=ps, l=lhsT, r=rhs, st=start, sp=stop: e.matmul(o, l, r, start=st, stop=sp),
                  waits, incs)

    def trans(self, ps, in_, idn, waits=(), incs=()):
        self.emit("tensor", lambda e, o=ps, i=in_, d=idn: e.transpose(o, i, d), waits, incs)

    def copy(self, eng, out, in_, waits=(), incs=()):
        self.emit(eng, lambda e, o=out, i=in_: e.tensor_copy(o, i), waits, incs)

    def tt(self, eng, out, a, b, op, waits=(), incs=()):
        self.emit(eng, lambda e, o=out, x=a, y=b, p=op: e.tensor_tensor(o, x, y, p), waits, incs)

    def ts(self, eng, out, a, s1, s2, op0, op1=ALU.bypass, waits=(), incs=()):
        self.emit(eng, lambda e, o=out, x=a, u=s1, v=s2, p0=op0, p1=op1:
                  e.tensor_scalar(o, x, u, v, p0, p1), waits, incs)

    def stt(self, eng, out, in0, scalar, in1, op0, op1, waits=(), incs=()):
        self.emit(eng, lambda e, o=out, x=in0, s=scalar, y=in1, p0=op0, p1=op1:
                  e.scalar_tensor_tensor(o, x, s, y, p0, p1), waits, incs)

    def act(self, out, in_, func, bias=0.0, scale=1.0, accum=None, waits=(), incs=()):
        self.emit("scalar", lambda e, o=out, i=in_, f=func, b=bias, s=scale, a=accum:
                  e.activation(o, i, f, bias=b, scale=s, accum_out=a), waits, incs)

    def red(self, eng, out, in_, op, waits=(), incs=()):
        self.emit(eng, lambda e, o=out, i=in_, p=op:
                  e.tensor_reduce(o, i, AX.X, p, apply_absolute_value=False), waits, incs)

    def recip(self, out, in_, waits=(), incs=()):
        self.emit("vector", lambda e, o=out, i=in_: e.reciprocal(o, i), waits, incs)

    def scan(self, out, d0, d1, init, op0, op1, waits=(), incs=()):
        self.emit("vector", lambda e, o=out, a=d0, b=d1, iv=init, p0=op0, p1=op1:
                  e.tensor_tensor_scan(o, a, b, iv, p0, p1), waits, incs)

    def memset(self, eng, ap, val, waits=(), incs=()):
        self.emit(eng, lambda e, a=ap, v=val: e.memset(a, v), waits, incs)

    def iota(self, ap, pattern, base, chanmul, waits=(), incs=()):
        self.emit("gpsimd", lambda e, a=ap, p=pattern, b=base, c=chanmul:
                  e.iota(a, p, base=b, channel_multiplier=c), waits, incs)

    def wait_only(self, eng, waits):
        for s, v in waits:
            if self.waited[eng].get(s.uid, -1) < v:
                self.waited[eng][s.uid] = v
                self.streams[eng].append(lambda e, h=s.h, vv=v: e.wait_ge(h, vv))

    def finalize(self):
        bad = []
        for uid, (s, v) in self.maxwait.items():
            if v > s.n:
                bad.append(f"{s.name}: wait {v} > total incs {s.n}")
        assert not bad, "unreachable waits: " + "; ".join(bad)
        with self.nc.Block() as block:
            for eng in ENGS:
                fns = self.streams[eng]
                if fns:
                    getattr(block, eng)(lambda e, fns=fns: [f(e) for f in fns])


def tile_lhsT(W):
    K, M = W.shape
    assert K % 128 == 0 and M % 128 == 0
    return np.ascontiguousarray(
        W.reshape(K // 128, 128, M // 128, 128).transpose(2, 1, 0, 3).reshape(M // 128, 128, K))


def col_stage(v):
    n = v.shape[0] // 128
    return np.ascontiguousarray(v.reshape(n, 128).T)


class Builder:
    def __init__(self, layers, dev_stop="full"):
        self.layers = layers
        self.dev_stop = dev_stop
        nc = bass.Bass("TRN2", target_bir_lowering=False)
        self.nc = nc
        self.stack = ExitStack()
        self.kb = KB(nc, self.stack)
        self.WD = {}
        self.x_in = nc.declare_dram_parameter("x", [D, T], F32, isOutput=False).ap()
        self.y_out = nc.declare_dram_parameter("y", [D, T], F32, isOutput=True).ap()
        self.dram_cache = {}

        def sb(name, shape, dtype):
            return nc.alloc_sbuf_tensor(name, list(shape), dtype).ap()
        self.ph = sb("pool_h", [128, 32 * 1024], F32R)
        self.pbr = sb("pool_br", [128, 9472], F32R)
        self.pbf = sb("pool_bf", [128, 3072], F32)
        self.pw = sb("pool_w", [128, 4 * 1024], F32R)
        self.ps_ = sb("pool_s", [128, 2 * 1024], F32)
        self.pc = sb("pool_c", [128, 768], F32)
        self.ones_r = sb("ones_r", [1, 128], F32R)
        self.onec_r = sb("onec_r", [128, 1], F32R)
        self.ones2_r = sb("ones2_r", [128, 2], F32R)
        self.ident = sb("ident", [128, 128], F32)
        self.ident2 = sb("ident2", [128, 64], F32)
        self.epsc = sb("epsc", [128, 8], F32)
        self.psb = [nc.alloc_psum_tensor(f"pbk{i}", [128, 512], F32).ap() for i in range(8)]
        self.bmark = {i: () for i in range(8)}
        self.pc_off = 328
        self._semcache = {}

    def S(self, name):
        if name not in self._semcache:
            self._semcache[name] = self.kb.sem(name)
        return self._semcache[name]

    def SL(self, name, n):
        return [self.S(f"{name}{i}") for i in range(n)]

    def pc_load(self, wname, ncols, sem=None):
        cache = getattr(self, "pc_cache", None)
        if cache and wname in cache:
            return cache[wname]
        off = self.pc_off
        self.pc_off += ncols
        assert self.pc_off <= 768, "pool_c overflow"
        ap = self.pc[:, off:off + ncols]
        s = self.kb.sem()
        self.kb.dma("sync", ap, self.WD[wname][:, :], sem=s)
        return ap, [(s, 16)]

    def pc_preload(self):
        specs = []
        if 2 in self.layers:
            specs += [("L2_bg_c", 8), ("L2_gcol_c", 16), ("L2_swb_rep", 8),
                      ("L2_lng_c", 8), ("L2_lnb_c", 8), ("L2_bo_c", 8)]
        if 3 in self.layers:
            specs += [("L3_bqkv_c", 24), ("L3_swb_rep", 8), ("L3_bproj_c", 8)]
        if not specs:
            return
        s = self.kb.sem()
        aps = {}
        for wname, ncols in specs:
            off = self.pc_off
            self.pc_off += ncols
            assert self.pc_off <= 768, "pool_c overflow"
            aps[wname] = self.pc[:, off:off + ncols]
            self.kb.dma("sync", aps[wname], self.WD[wname][:, :], sem=s)
        wv = [(s, s.n)]
        self.pc_cache = {k: (ap, wv) for k, ap in aps.items()}

    def decl_weights(self, meta):
        for name, (shape, dtype) in meta.items():
            self.WD[name] = self.nc.declare_dram_parameter(name, list(shape), dtype, isOutput=False).ap()

    def dram(self, name, shape, dtype=F32):
        if name not in self.dram_cache:
            self.dram_cache[name] = self.nc.dram_tensor(name, list(shape), dtype).ap()
        return self.dram_cache[name]

    def hu(self, u):
        return self.ph[:, u * 512:(u + 1) * 512]

    def bw(self, bk):
        return list(self.bmark[bk])

    def init_consts(self):
        kb = self.kb
        sI = kb.sem("init")
        kb.dma("sync", self.ones_r[:, :], self.WD["c_ones_row"][:, :].bitcast(F32R), sem=sI)
        kb.dma("sync", self.onec_r[:, :], self.WD["c_ones_col"][:, :].bitcast(F32R), sem=sI)
        kb.dma("sync", self.ident[:, :], self.WD["c_ident"][:, :], sem=sI)
        kb.dma("sync", self.ident2[:, :], self.WD["c_ident2"][:, :], sem=sI)
        kb.dma("sync", self.epsc[:, :], self.WD["c_eps"][:, :], sem=sI)
        if "c_ones2" in self.WD:
            kb.dma("sync", self.ones2_r[:, :], self.WD["c_ones2"][:, :].bitcast(F32R), sem=sI)
        self.init_w = [(sI, sI.n)]

    def phase_ln(self, x_cur, xn_h, entry):
        kb, psb = self.kb, self.psb
        s_x = self.SL("ln_x", 2)
        s_sq = self.S("ln_sq"); s_mm = self.S("ln_mm"); s_row = self.S("ln_row")
        s_acr = self.S("ln_acr"); s_bc = self.S("ln_bc"); s_nm = self.S("ln_nm")
        s_st = self.SL("ln_st", 2)
        nm0 = s_nm.n; bc0 = s_bc.n
        st0 = [s_st[0].n, s_st[1].n]
        self._ln_st0 = st0
        xt = [[self.hu(r * 24 + d) for d in range(DB)] for r in range(2)]
        xq = [[self.hu(r * 24 + 8 + d) for d in range(DB)] for r in range(2)]
        xnt = [[self.hu(r * 24 + 16 + d) for d in range(DB)] for r in range(2)]
        mean = self.ps_[0:1, 0:512]
        q = self.ps_[0:1, 512:1024]
        var = self.ps_[0:1, 1024:1536]
        lnv = self.ps_[0:1, 1536:2048]
        nmean_r = self.pbr[0:1, 8448:8960]
        rstd_r = self.pbr[0:1, 8960:9472]
        for t in range(NT):
            r = t % 2
            guard = [(s_nm, nm0 + 16 * (t - 1))] if t >= 2 else []
            for d in range(DB):
                kb.dma("sync", xt[r][d],
                       x_cur[d * 128:(d + 1) * 128, t * 512:(t + 1) * 512].bitcast(F32R),
                       waits=entry + guard, sem=s_x[r])
            xw = [(s_x[r], s_x[r].n)]
            for d in range(DB):
                kb.act(xq[r][d], xt[r][d], AF.Square,
                       waits=entry + guard + xw, incs=[(s_sq, 1)])
            psS = psb[r][0:1, 0:512]
            psQ = psb[2 + r][0:1, 0:512]
            for d in range(DB):
                kb.mm(psS, self.onec_r, xt[r][d], d == 0, d == DB - 1,
                      waits=entry + self.init_w + self.bw(r) + xw,
                      incs=[(s_mm, 1)] if d == DB - 1 else ())
            for d in range(DB):
                kb.mm(psQ, self.onec_r, xq[r][d], d == 0, d == DB - 1,
                      waits=self.bw(2 + r) + [(s_sq, s_sq.n)],
                      incs=[(s_mm, 1)] if d == DB - 1 else ())
            mmv = s_mm.n
            row_g = [(s_bc, bc0 + 2 * t)] if t >= 1 else []
            kb.ts("vector", mean, psS, 1.0 / D, None, ALU.mult,
                  waits=[(s_mm, mmv - 1)] + row_g, incs=[(s_row, 1)])
            kb.ts("vector", q, psQ, 1.0 / D, None, ALU.mult,
                  waits=[(s_mm, mmv)], incs=[(s_row, 1)])
            rv = s_row.n
            self.bmark[r] = [(s_row, rv)]
            self.bmark[2 + r] = [(s_row, rv)]
            kb.tt("vector", var, mean, mean, ALU.mult,
                  waits=[(s_row, rv), (s_acr, s_acr.n)], incs=[(s_row, 1)])
            kb.tt("vector", var, q, var, ALU.subtract, waits=[(s_row, s_row.n)], incs=[(s_row, 1)])
            kb.ts("vector", nmean_r, mean, -1.0, None, ALU.mult, waits=row_g, incs=[(s_row, 1)])
            kb.act(lnv, var, AF.Ln, bias=self.epsc[0:1, 0:1],
                   waits=row_g + self.init_w + [(s_row, rv + 2)], incs=[(s_acr, 1)])
            kb.act(rstd_r, lnv, AF.Exp, scale=-0.5, waits=[(s_acr, s_acr.n)], incs=[(s_acr, 1)])
            psMB = psb[4][:, 0:512]
            psRB = psb[5][:, 0:512]
            kb.mm(psMB, self.ones_r, nmean_r, True, True,
                  waits=self.bw(4) + [(s_row, s_row.n)], incs=[(s_bc, 1)])
            kb.mm(psRB, self.ones_r, rstd_r, True, True,
                  waits=self.bw(5) + [(s_acr, s_acr.n)], incs=[(s_bc, 1)])
            bcv = s_bc.n
            st_g = [(s_st[r], st0[r] + 16 * 8 * (t // 2))] if t >= 2 else []
            nmt = s_nm.n
            for d in range(DB):
                kb.tt("vector", xnt[r][d], xt[r][d], psMB, ALU.add,
                      waits=[(s_bc, bcv)] + st_g, incs=[(s_nm, 1)])
            for d in range(DB):
                kb.tt("vector", xnt[r][d], xnt[r][d], psRB, ALU.mult,
                      waits=[(s_nm, nmt + d + 1)], incs=[(s_nm, 1)])
                kb.dma("gpsimd", xn_h[d * 128:(d + 1) * 128, t * 512:(t + 1) * 512], xnt[r][d],
                       waits=[(s_nm, s_nm.n)], sem=s_st[r])
            self.bmark[4] = [(s_nm, s_nm.n)]
            self.bmark[5] = [(s_nm, s_nm.n)]
        return s_st

    def ln_chunk_w(self, s_st, t):
        return [(s_st[t % 2], self._ln_st0[t % 2] + 16 * 8 * (t // 2 + 1))]

    def ln_full_w(self, s_st):
        return [(s_st[0], s_st[0].n), (s_st[1], s_st[1].n)]

    def stream_proj(self, wt_ap, n_do, n_k, rhs_fn, evac_fn, entry, wsems, mmsem,
                    nfree=512, kcols=128, banks=(6, 7), ring=4, queue="sync"):
        kb, psb = self.kb, self.psb
        slab = n_k * kcols
        assert ring * slab <= 4 * 1024
        assert len(wsems) >= ring
        wbuf = [self.pw[:, i * slab:(i + 1) * slab] for i in range(ring)]
        freew = [list(entry)] * ring
        for do in range(n_do):
            rr = do % ring
            kb.dma(queue, wbuf[rr], wt_ap[do].bitcast(F32R), waits=list(freew[rr]) + entry + self.pw_guard(),
                   sem=wsems[rr])
            wv = wsems[rr].n
            bk = banks[do % len(banks)]
            ps = psb[bk][0:128, 0:nfree]
            for k in range(n_k):
                kb.mm(ps, wbuf[rr][:, k * kcols:(k + 1) * kcols], rhs_fn(k), k == 0, k == n_k - 1,
                      waits=entry + self.bw(bk) + [(wsems[rr], wv)],
                      incs=[(mmsem, 1)] if k == n_k - 1 else ())
            ev = evac_fn(do, ps, [(mmsem, mmsem.n)])
            self.bmark[bk] = ev
            freew[rr] = [(mmsem, mmsem.n)]
        self._last_pw_user = [(mmsem, mmsem.n)]
        return mmsem

    def pw_guard(self):
        return list(getattr(self, "_last_pw_user", ()))

    def layer0(self, x_cur, x_next):
        kb, psb, WD = self.kb, self.psb, self.WD
        xn_h = self.dram("xn0", [D, T], F32R)
        xm_h = self.dram("xm0", [D, T], F32)
        s_ln = self.phase_ln(x_cur, xn_h, [])
        if self.dev_stop == "ln":
            s_cp = kb.sem()
            kb.dma("sync", x_next[:, :], xn_h[:, :].bitcast(F32),
                   waits=self.ln_full_w(s_ln), sem=s_cp)
            return s_cp

        s_xn = self.S("a_xn"); s_pe = self.S("a_pe"); s_dv = self.S("a_dv"); s_ac = self.S("a_ac")
        s_qkv = self.S("a_qkv"); s_vt = self.S("a_vt"); s_ctx = self.S("a_ctx"); s_asm = self.S("a_asm")
        s_xl = [self.S("a_xl"), self.S("q_x")]; s_stm = self.SL("a_stm", 2); s_sum = self.S("a_sum")
        s_ws = self.SL("w_s", 4); s_mmp = self.S("w_mm")
        import os

        xn_t = [self.pbr[:, d * 512:(d + 1) * 512] for d in range(DB)]
        K_t = [self.pbr[:, 4096 + d * 512:4096 + (d + 1) * 512] for d in range(DB)]
        xtile = [self.pbf[:, i * 512:(i + 1) * 512] for i in range(2)]
        a1 = [self.pbf[:, 1024 + i * 512:1024 + (i + 1) * 512] for i in range(2)]
        summ = [self.pbr[:, 8192 + d * 16:8192 + (d + 1) * 16] for d in range(DB)]
        gout = [self.pc[:, 160 + d * 16:176 + d * 16] for d in range(DB)]

        prev_t_done = []
        for t in range(NT):
            r = t % 2
            Qt = [self.hu(r * 24 + d) for d in range(DB)]
            ctx = [self.hu(r * 24 + 8 + d) for d in range(DB)]

            def vT_ap(c, h, kb_, r=r):
                bid = c * 32 + h * 2 + kb_
                u = r * 24 + 16 + bid // 8
                col = (bid % 8) * 64
                return self.ph[:, u * 512 + col:u * 512 + col + 64]
            guard = list(prev_t_done) + self.ln_full_w(s_ln)
            for d in range(DB):
                kb.dma("scalar", xn_t[d], xn_h[d * 128:(d + 1) * 128, t * 512:(t + 1) * 512],
                       waits=guard, sem=s_xn)
            xn_w = [(s_xn, s_xn.n)]

            def evac_q(do, ps, w, Qt=Qt):
                kb.copy("vector", Qt[do], ps, waits=w, incs=[(s_qkv, 1)])
                return [(s_qkv, s_qkv.n)]
            self.stream_proj(WD["L0_Wq_l_t"], DB, DB, lambda k, x=xn_t: x[k], evac_q,
                             guard + xn_w + self.init_w, s_ws, s_mmp)

            def evac_k(do, ps, w, K_t=K_t):
                kb.copy("vector", K_t[do], ps, waits=w, incs=[(s_qkv, 1)])
                return [(s_qkv, s_qkv.n)]
            self.stream_proj(WD["L0_Wk_l_t"], DB, DB, lambda k, x=xn_t: x[k], evac_k,
                             guard + xn_w, s_ws, s_mmp)

            vscr = self.hu(54 + r)

            def evac_v(do, ps, w, vscr=vscr, r=r):
                kb.copy("vector", vscr, ps, waits=w + [(s_vt, s_vt.n)], incs=[(s_dv, 1)])
                dvv = s_dv.n
                for hp_i in range(2):
                    for c in range(2):
                        for kb_ in range(2):
                            h = do * 2 + hp_i
                            bk = 4 + (do % 2)
                            pt = psb[bk][0:128, 0:64]
                            kb.trans(pt, vscr[hp_i * 64:hp_i * 64 + 64,
                                              c * 256 + kb_ * 128:c * 256 + (kb_ + 1) * 128].bitcast(F32),
                                     self.ident2[hp_i * 64:hp_i * 64 + 64, 0:64],
                                     waits=[(s_dv, dvv)] + self.bw(bk) + self.init_w, incs=[(s_pe, 1)])
                            kb.copy("vector", vT_ap(c, h, kb_), pt.bitcast(F32R),
                                    waits=[(s_pe, s_pe.n)], incs=[(s_vt, 1)])
                            self.bmark[bk] = [(s_vt, s_vt.n)]
                return [(s_vt, s_vt.n)]
            self.stream_proj(WD["L0_Wv_l_t"], DB, DB, lambda k, x=xn_t: x[k], evac_v,
                             guard + xn_w, s_ws, s_mmp)
            qkv_done = [(s_qkv, s_qkv.n), (s_vt, s_vt.n)]

            probs = [self.hu(48 + i) for i in range(2)]
            probsT = [[self.ph[:, (50 + 2 * i + kb_) * 512:(50 + 2 * i + kb_) * 512 + 256]
                       for kb_ in range(2)] for i in range(2)]
            mrow = [self.ps_[:, 1024 + i:1025 + i] for i in range(2)]
            nrow = [self.ps_[:, 1028 + i:1029 + i] for i in range(2)]
            srow = [self.ps_[:, 1032 + i:1033 + i] for i in range(2)]
            rrow = [self.ps_[:, 1036 + i:1037 + i] for i in range(2)]
            probs_mark = [(), ()]
            import os
            nunits = int(os.environ.get("KUNITS", "32"))
            for c in range(2):
                for h in range(16):
                    if c * 16 + h >= nunits:
                        break
                    u2 = (c * 16 + h) % 2
                    hd, hp = h // 2, (h % 2) * 64
                    for qh in range(2):
                        prq = probs[u2][:, qh * 256:(qh + 1) * 256]
                        ps_s = psb[qh][:, 0:256]
                        kb.mm(ps_s, Qt[hd][hp:hp + 64, c * 256 + qh * 128:c * 256 + (qh + 1) * 128],
                              K_t[hd][hp:hp + 64, c * 256:(c + 1) * 256], True, True,
                              waits=qkv_done + self.bw(qh), incs=[(s_pe, 1)])
                        pev = s_pe.n
                        kb.red("vector", mrow[qh], ps_s, ALU.max, waits=[(s_pe, pev)], incs=[(s_dv, 1)])
                        kb.ts("vector", nrow[qh], mrow[qh], -0.125, None, ALU.mult,
                              waits=[(s_dv, s_dv.n)], incs=[(s_dv, 1)])
                        kb.act(prq, ps_s, AF.Exp, bias=nrow[qh], scale=0.125,
                               waits=[(s_dv, s_dv.n)] + list(probs_mark[u2]), incs=[(s_ac, 1)])
                        self.bmark[qh] = [(s_ac, s_ac.n)]
                        import os as _os
                        _mode = _os.environ.get("KNO_NORM", "0")
                        if _mode == "0":
                            kb.red("vector", srow[qh], prq, ALU.add, waits=[(s_ac, s_ac.n)], incs=[(s_dv, 1)])
                            kb.recip(rrow[qh], srow[qh], waits=[(s_dv, s_dv.n)], incs=[(s_dv, 1)])
                            kb.ts("vector", prq, prq, rrow[qh], None, ALU.mult,
                                  waits=[(s_dv, s_dv.n)], incs=[(s_dv, 1)])
                        elif _mode == "2":
                            kb.red("vector", srow[qh], prq, ALU.add, waits=[(s_ac, s_ac.n)], incs=[(s_dv, 1)])
                        elif _mode == "3":
                            kb.red("vector", srow[qh], prq.bitcast(F32), ALU.add, waits=[(s_ac, s_ac.n)], incs=[(s_dv, 1)])
                        else:
                            kb.wait_only("vector", [(s_ac, s_ac.n)])
                        for kb_ in range(2):
                            tb = 2 + kb_
                            pt = psb[tb][:, 0:128]
                            kb.trans(pt, prq[:, kb_ * 128:(kb_ + 1) * 128].bitcast(F32), self.ident,
                                     waits=[(s_dv, s_dv.n)] + self.bw(tb), incs=[(s_pe, 1)])
                            kb.copy("vector", probsT[u2][kb_][:, qh * 128:(qh + 1) * 128],
                                    pt.bitcast(F32R), waits=[(s_pe, s_pe.n)], incs=[(s_dv, 1)])
                            self.bmark[tb] = [(s_dv, s_dv.n)]
                    probs_mark[u2] = [(s_dv, s_dv.n)]
                    ps_c = psb[4][0:64, 0:256]
                    trv = s_dv.n
                    for kb_ in range(2):
                        kb.mm(ps_c, vT_ap(c, h, kb_), probsT[u2][kb_], kb_ == 0, kb_ == 1,
                              waits=[(s_dv, trv)] + qkv_done + self.bw(4), incs=[(s_pe, 1)])
                    kb.copy("vector", ctx[hd][hp:hp + 64, c * 256:(c + 1) * 256], psb[4][0:64, 0:256],
                            waits=[(s_pe, s_pe.n)], incs=[(s_ctx, 1)])
                    self.bmark[4] = [(s_ctx, s_ctx.n)]
            ctx_done = [(s_ctx, s_ctx.n)]
            if self.dev_stop == "u0":
                s_cp = kb.sem()
                kb.copy("vector", self.pbf[:, 0:256], psb[0][:, 0:256], waits=ctx_done, incs=[(s_dv, 1)])
                kb.copy("vector", self.pbf[:, 256:512], psb[1][:, 0:256], waits=[(s_dv, s_dv.n)], incs=[(s_dv, 1)])
                kb.dma("sync", x_next[0:128, 2560:3072], self.pbf[:, 0:512], waits=[(s_dv, s_dv.n)], sem=s_cp)
                kb.dma("sync", x_next[0:128, 3072:3088], self.ps_[:, 1024:1040], waits=ctx_done, sem=s_cp)
                kb.dma("sync", x_next[0:128, 0:512], probs[0].bitcast(F32), waits=ctx_done, sem=s_cp)
                kb.dma("sync", x_next[0:128, 512:768], probsT[0][0].bitcast(F32), waits=ctx_done, sem=s_cp)
                kb.dma("sync", x_next[0:128, 768:1024], probsT[0][1].bitcast(F32), waits=ctx_done, sem=s_cp)
                kb.dma("sync", x_next[0:128, 1024:1536], self.hu(16).bitcast(F32), waits=ctx_done, sem=s_cp)
                kb.dma("sync", x_next[0:128, 1536:2048], self.hu(54).bitcast(F32), waits=ctx_done, sem=s_cp)
                for d in range(DB):
                    kb.dma("sync", x_next[d * 128:(d + 1) * 128, 2048:2560], ctx[d].bitcast(F32),
                           waits=ctx_done, sem=s_cp)
                return s_cp

            def evac_wo(do, ps, w, t=t):
                i2 = do % 2
                kb.dma("scalar", xtile[i2],
                       x_cur[do * 128:(do + 1) * 128, t * 512:(t + 1) * 512],
                       waits=[(s_asm, max(0, s_asm.n - 1))], sem=s_xl[i2])
                xlv = s_xl[i2].n
                kb.tt("vector", a1[i2], xn_t[do], ps, ALU.add,
                      waits=w + [(s_stm[i2], s_stm[i2].n)], incs=[(s_dv, 1)])
                sA = self.ps_[:, 1040 + 2 * i2:1041 + 2 * i2]
                sB = self.ps_[:, 1041 + 2 * i2:1042 + 2 * i2]
                for c in range(2):
                    col = t * 2 + c
                    kb.red("vector", sA, xn_t[do][:, c * 256:(c + 1) * 256],
                           ALU.add, waits=w + [(s_sum, s_sum.n)], incs=[(s_sum, 1)])
                    kb.red("vector", sB, ps[:, c * 256:(c + 1) * 256],
                           ALU.add, waits=w + [(s_sum, s_sum.n)], incs=[(s_sum, 1)])
                    kb.tt("vector", sA, sA, sB, ALU.add, waits=[(s_sum, s_sum.n)], incs=[(s_sum, 1)])
                    kb.ts("vector", summ[do][:, col:col + 1], sA,
                          1.0 / 256, None, ALU.mult, waits=[(s_sum, s_sum.n)], incs=[(s_sum, 1)])
                kb.tt("vector", a1[i2], a1[i2], xtile[i2], ALU.add,
                      waits=[(s_xl[i2], xlv), (s_sum, s_sum.n)], incs=[(s_asm, 1)])
                kb.dma("gpsimd", xm_h[do * 128:(do + 1) * 128, t * 512:(t + 1) * 512], a1[i2],
                       waits=[(s_asm, s_asm.n)], sem=s_stm[i2])
                return [(s_asm, s_asm.n)]
            self.stream_proj(WD["L0_Wo_l_t"], DB, DB, lambda k, ctx=ctx: ctx[k], evac_wo,
                             ctx_done, s_ws, s_mmp)
            prev_t_done = [(s_asm, s_asm.n)]
            if self.dev_stop == "t0" and t == 0:
                s_cp = kb.sem()
                for d in range(DB):
                    kb.dma("sync", x_next[d * 128:(d + 1) * 128, 0:512], Qt[d].bitcast(F32),
                           waits=[(s_asm, s_asm.n)], sem=s_cp)
                    kb.dma("sync", x_next[d * 128:(d + 1) * 128, 512:1024], K_t[d].bitcast(F32),
                           waits=[(s_asm, s_asm.n)], sem=s_cp)
                    kb.dma("sync", x_next[d * 128:(d + 1) * 128, 1024:1536], ctx[d].bitcast(F32),
                           waits=[(s_asm, s_asm.n)], sem=s_cp)
                    kb.dma("sync", x_next[d * 128:(d + 1) * 128, 1536:2048],
                           self.ph[:, (16 + d) * 512:(17 + d) * 512].bitcast(F32),
                           waits=[(s_asm, s_asm.n)], sem=s_cp)
                kb.dma("sync", x_next[0:128, 2048:2560], probs[0].bitcast(F32),
                       waits=[(s_asm, s_asm.n)], sem=s_cp)
                kb.dma("sync", x_next[0:128, 2560:3072], probs[1].bitcast(F32),
                       waits=[(s_asm, s_asm.n)], sem=s_cp)
                return s_cp
        if self.dev_stop == "attn":
            s_cp = kb.sem()
            kb.dma("sync", x_next[:, :], xm_h[:, :],
                   waits=[(s_stm[0], s_stm[0].n), (s_stm[1], s_stm[1].n)], sem=s_cp)
            return s_cp

        s_g = kb.sem()
        sum_done = [(s_sum, s_sum.n), (s_asm, s_asm.n)]
        Qg = [self.pbr[:, d * 16:(d + 1) * 16] for d in range(DB)]
        Kg = [self.pbr[:, 128 + d * 16:128 + (d + 1) * 16] for d in range(DB)]
        Vg = [self.pbr[:, 256 + d * 16:256 + (d + 1) * 16] for d in range(DB)]
        gct = [self.pbr[:, 384 + d * 16:384 + (d + 1) * 16] for d in range(DB)]
        for Wn, out in (("L0_Wq_g_t", Qg), ("L0_Wk_g_t", Kg), ("L0_Wv_g_t", Vg)):
            def evg(do, ps, w, out=out):
                kb.copy("vector", out[do], ps[:, 0:16], waits=w, incs=[(s_g, 1)])
                return [(s_g, s_g.n)]
            self.stream_proj(WD[Wn], DB, DB, lambda k: summ[k], evg, sum_done, s_ws, s_mmp, nfree=16)
        gq = [(s_g, s_g.n)]
        mrow = self.ps_[0:8, 1100:1101]; nrow = self.ps_[0:8, 1101:1102]
        srow = self.ps_[0:8, 1102:1103]; rrow = self.ps_[0:8, 1103:1104]
        prg = self.pbr[0:8, 512:520]
        prTg = self.pbr[0:8, 520:528]
        vTg = self.pbr[0:8, 528:592]
        for b in range(BPC):
            for h in range(16):
                hd, hp = h // 2, (h % 2) * 64
                ps_s = psb[0][0:8, 0:8]
                kb.mm(ps_s, Qg[hd][hp:hp + 64, b * 8:(b + 1) * 8], Kg[hd][hp:hp + 64, b * 8:(b + 1) * 8],
                      True, True, waits=gq + self.bw(0), incs=[(s_pe, 1)])
                kb.red("vector", mrow, ps_s, ALU.max, waits=[(s_pe, s_pe.n)], incs=[(s_dv, 1)])
                kb.ts("vector", nrow, mrow, -0.125, None, ALU.mult, waits=[(s_dv, s_dv.n)], incs=[(s_dv, 1)])
                kb.act(prg, ps_s, AF.Exp, bias=nrow, scale=0.125, accum=srow,
                       waits=[(s_dv, s_dv.n)], incs=[(s_ac, 1)])
                self.bmark[0] = [(s_ac, s_ac.n)]
                kb.recip(rrow, srow, waits=[(s_ac, s_ac.n)], incs=[(s_dv, 1)])
                kb.ts("vector", prg, prg, rrow, None, ALU.mult, waits=[(s_dv, s_dv.n)], incs=[(s_dv, 1)])
                pt = psb[1][0:8, 0:8]
                kb.trans(pt, prg.bitcast(F32), self.ident[0:8, 0:8],
                         waits=[(s_dv, s_dv.n)] + self.bw(1), incs=[(s_pe, 1)])
                kb.copy("vector", prTg, pt.bitcast(F32R), waits=[(s_pe, s_pe.n)], incs=[(s_dv, 1)])
                self.bmark[1] = [(s_dv, s_dv.n)]
                pv = psb[2][0:8, 0:64]
                kb.trans(pv, Vg[hd][hp:hp + 64, b * 8:(b + 1) * 8].bitcast(F32), self.ident2[hp:hp + 64, 0:64],
                         waits=gq + [(s_dv, s_dv.n)] + self.bw(2), incs=[(s_pe, 1)])
                kb.copy("vector", vTg, pv.bitcast(F32R), waits=[(s_pe, s_pe.n)], incs=[(s_dv, 1)])
                self.bmark[2] = [(s_dv, s_dv.n)]
                pc_ = psb[3][0:64, 0:8]
                kb.mm(pc_, vTg, prTg, True, True, waits=[(s_dv, s_dv.n)] + self.bw(3), incs=[(s_pe, 1)])
                kb.copy("vector", gct[hd][hp:hp + 64, b * 8:(b + 1) * 8], psb[3][0:64, 0:8],
                        waits=[(s_pe, s_pe.n)], incs=[(s_g, 1)])
                self.bmark[3] = [(s_g, s_g.n)]
        gcd = [(s_g, s_g.n)]

        def evac_wog(do, ps, w):
            kb.tt("vector", gout[do], summ[do], ps[:, 0:16], ALU.add, waits=w, incs=[(s_g, 1)])
            return [(s_g, s_g.n)]
        self.stream_proj(WD["L0_Wo_g_t"], DB, DB, lambda k: gct[k], evac_wog, gcd,
                         s_ws, s_mmp, nfree=16)
        g_done = [(s_g, s_g.n)]
        if self.dev_stop == "global":
            s_cp = kb.sem()
            for d in range(DB):
                kb.dma("sync", x_next[d * 128:(d + 1) * 128, 0:16], gout[d], waits=g_done, sem=s_cp)
                kb.dma("sync", x_next[d * 128:(d + 1) * 128, 16:32], summ[d].bitcast(F32), waits=g_done, sem=s_cp)
            return s_cp

        return self.ffn_relu(xn_h, xm_h, x_next,
                             g_done + [(s_stm[0], s_stm[0].n), (s_stm[1], s_stm[1].n)],
                             "L0_W1_t", "L0_b1_c", "L0_W2_t", "L0_b2_c", gout=gout)

    def ffn_relu(self, xn_h, xm_h, x_next, entry, w1n, b1n, w2n, b2n, gout=None):
        kb, psb, WD = self.kb, self.psb, self.WD
        s_hx = self.SL("f_hx", 2); s_w1 = self.SL("w_s", 4)
        s_w2 = self.SL("f_w2", 2); s_mmf = self.S("f_mm")
        s_h = self.S("f_h"); s_xm = self.SL("f_xm", 2); s_ev = self.S("f_ev")
        s_ev2 = self.S("f_ev2"); s_fst = self.SL("f_st", 2)
        TS = 1024
        NSC = T // TS
        b1sb = self.pc[:, 288:320]
        b2sb = self.pc[:, 320:328]
        s_cl = kb.sem()
        kb.dma("sync", b1sb, WD[b1n][:, :], waits=entry, sem=s_cl)
        kb.dma("sync", b2sb, WD[b2n][:, :], sem=s_cl)
        cl_w = [(s_cl, 32)]
        h_t = [self.ph[:, k * 1024:(k + 1) * 1024] for k in range(32)]
        xn_t = [self.pbr[:, d * 1024:(d + 1) * 1024] for d in range(DB)]
        xmt = [self.pbf[:, i * 1024:(i + 1) * 1024] for i in range(2)]
        o1 = [self.pbf[:, 2048 + i * 512:2048 + (i + 1) * 512] for i in range(2)]
        w1buf = [self.pw[:, i * 1024:(i + 1) * 1024] for i in range(4)]
        w2buf = [self.pw[:, i * 2048:(i + 1) * 2048] for i in range(2)]
        mm_sc_end = 0
        freew = [list(entry) + self.pw_guard()] * 4
        w2free = [[], []]
        for scn in range(NSC):
            hguard = [(s_mmf, mm_sc_end)] if scn >= 1 else []
            sx = s_hx[scn % 2]
            for d in range(DB):
                kb.dma("scalar", xn_t[d], xn_h[d * 128:(d + 1) * 128, scn * TS:(scn + 1) * TS],
                       waits=entry + hguard, sem=sx)
            xnw = [(sx, sx.n)]
            for do1 in range(32):
                rr = do1 % 4
                kb.dma("sync", w1buf[rr], WD[w1n][do1].bitcast(F32R),
                       waits=list(freew[rr]) + entry + hguard + list(w2free[rr % 2]), sem=s_w1[rr])
                wv = s_w1[rr].n
                for th in range(2):
                    bk = (do1 * 2 + th) % 4
                    ps = psb[bk][:, 0:512]
                    for k in range(DB):
                        kb.mm(ps, w1buf[rr][:, k * 128:(k + 1) * 128], xn_t[k][:, th * 512:(th + 1) * 512],
                              k == 0, k == DB - 1,
                              waits=xnw + hguard + self.bw(bk) + [(s_w1[rr], wv)],
                              incs=[(s_mmf, 1)] if k == DB - 1 else ())
                    kb.act(h_t[do1][:, th * 512:(th + 1) * 512], ps, AF.Relu, bias=b1sb[:, do1:do1 + 1],
                           waits=cl_w + [(s_mmf, s_mmf.n)], incs=[(s_h, 1)])
                    self.bmark[bk] = [(s_h, s_h.n)]
                freew[rr] = [(s_mmf, s_mmf.n)]
            h_done = [(s_h, s_h.n)]
            for do2 in range(DB):
                wv2 = []
                for half in range(2):
                    kb.dma("sync", w2buf[half], WD[w2n][do2][:, half * 2048:(half + 1) * 2048].bitcast(F32R),
                           waits=list(w2free[half]) + h_done, sem=s_w2[half])
                    wv2.append(s_w2[half].n)
                i2 = do2 % 2
                kb.dma("scalar", xmt[i2], xm_h[do2 * 128:(do2 + 1) * 128, scn * TS:(scn + 1) * TS],
                       waits=entry + [(s_ev, max(0, s_ev.n))], sem=s_xm[i2])
                xmv = s_xm[i2].n
                for th in range(2):
                    bk = 4 + th
                    ps = psb[bk][:, 0:512]
                    for k2 in range(32):
                        kb.mm(ps, w2buf[k2 // 16][:, (k2 % 16) * 128:(k2 % 16 + 1) * 128],
                              h_t[k2][:, th * 512:(th + 1) * 512], k2 == 0, k2 == 31,
                              waits=h_done + self.bw(bk) + [(s_w2[k2 // 16], wv2[k2 // 16])],
                              incs=[(s_mmf, 1)] if k2 == 31 else ())
                    mmv = s_mmf.n
                    tc0 = scn * TS + th * 512
                    for cc in range(2):
                        col = (tc0 + cc * 256) // 256
                        if gout is not None:
                            kb.ts("vector", o1[th][:, cc * 256:(cc + 1) * 256], ps[:, cc * 256:(cc + 1) * 256],
                                  b2sb[:, do2:do2 + 1], gout[do2][:, col:col + 1], ALU.add, ALU.add,
                                  waits=cl_w + entry + [(s_mmf, mmv), (s_fst[th], s_fst[th].n)],
                                  incs=[(s_ev, 1)])
                        else:
                            kb.ts("vector", o1[th][:, cc * 256:(cc + 1) * 256], ps[:, cc * 256:(cc + 1) * 256],
                                  b2sb[:, do2:do2 + 1], None, ALU.add,
                                  waits=cl_w + entry + [(s_mmf, mmv), (s_fst[th], s_fst[th].n)],
                                  incs=[(s_ev, 1)])
                    self.bmark[bk] = [(s_ev, s_ev.n)]
                    kb.tt("gpsimd", o1[th], o1[th], xmt[i2][:, th * 512:(th + 1) * 512], ALU.add,
                          waits=[(s_xm[i2], xmv), (s_ev, s_ev.n)], incs=[(s_ev2, 1)])
                    kb.dma("gpsimd", x_next[do2 * 128:(do2 + 1) * 128, tc0:tc0 + 512], o1[th],
                           waits=[(s_ev2, s_ev2.n)], sem=s_fst[th])
                w2free[0] = [(s_mmf, s_mmf.n)]
                w2free[1] = [(s_mmf, s_mmf.n)]
            mm_sc_end = s_mmf.n
        self._last_pw_user = [(s_mmf, s_mmf.n)]
        s_done = kb.sem()
        kb.emit("gpsimd", lambda e: e.engine_nop(),
                waits=[(s_fst[0], s_fst[0].n), (s_fst[1], s_fst[1].n)], incs=[(s_done, 1)])
        return s_done


    def layer1(self, x_cur, x_next, entry=()):
        kb, psb, WD = self.kb, self.psb, self.WD
        xn_h = self.dram("xn1", [D, T], F32R)
        xm_h = self.dram("xm1", [D, T], F32)
        xi_h = self.dram("xi1", [2048, T], F32R)
        u_h = self.dram("u1", [2048, T], F32R)
        sres_h = self.dram("sres1", [2048, T], F32)
        delta_h = self.dram("delta1", [2048, T], F32)
        ymix_h = self.dram("ymix1", [2048, T], F32R)
        wdq_h = self.dram("wdq1", [6, 8, 128, 1024], F32R)
        s_ln = self.phase_ln(x_cur, xn_h, list(entry))
        ln_full = self.ln_full_w(s_ln)
        if self.dev_stop == "ln":
            s_cp = kb.sem()
            kb.dma("sync", x_next[:, :], xn_h[:, :].bitcast(F32), waits=ln_full, sem=s_cp)
            return s_cp
        import os as _os0
        if _os0.environ.get("KQONLY"):
            self.qlora_ffn(xn_h, x_cur, xm_h, ln_full, wdq_h)
            s_cp = kb.sem()
            kb.dma("sync", x_next[:, :], xm_h[:, :], waits=[(self.s_ql, self.s_ql.n)], sem=s_cp)
            return s_cp

        s_xn = self.S("a_xn"); s_pe = self.S("a_pe"); s_dv = self.S("a_dv"); s_ac = self.S("a_ac")
        s_ev = self.S("a_ev"); s_ev2 = self.S("a_ev2"); s_ws = self.SL("w_s", 4)
        s_mmp = self.S("w_mm")
        s_sts = self.SL("st_s", 8)
        s_stc = self.SL("st_c", 2); s_std = self.S("st_d"); s_stq = self.SL("st_q", 2)
        s_ldc = self.SL("ld_c", 2); s_ldr = self.SL("ld_r", 4)
        s_wx = self.S("l1_wx"); s_wd = self.S("l1_wd"); s_ldq = self.SL("ld_q", 2)

        xn_t = [self.pbr[:, d * 512:(d + 1) * 512] for d in range(DB)]
        slot_mark = {}
        for t in range(NT):
            for d in range(DB):
                kb.dma("scalar", xn_t[d], xn_h[d * 128:(d + 1) * 128, t * 512:(t + 1) * 512],
                       waits=ln_full + ([(s_mmp, s_mmp.n)] if t else []), sem=s_xn)
            xn_w = [(s_xn, s_xn.n)]

            def evac_win(do, ps, w, t=t):
                slot = do % 8
                tile = self.hu(slot)
                g = list(slot_mark.get(slot, ()))
                if do < 16:
                    kb.copy("vector", tile, ps, waits=w + g, incs=[(s_ev, 1)])
                    ev = (s_ev, s_ev.n)
                    row0 = do * 128
                    dst = xi_h[row0:row0 + 128, t * 512:(t + 1) * 512]
                else:
                    kb.act(tile, ps, AF.Silu, waits=w + g, incs=[(s_ev2, 1)])
                    ev = (s_ev2, s_ev2.n)
                    row0 = (do - 16) * 128
                    dst = sres_h[row0:row0 + 128, t * 512:(t + 1) * 512].bitcast(F32R)
                kb.dma("gpsimd", dst, tile, waits=[ev], sem=s_sts[slot])
                slot_mark[slot] = [(s_sts[slot], s_sts[slot].n)]
                return [ev]
            self.stream_proj(WD["L1_W_in_t"], 32, DB, lambda k: xn_t[k], evac_win,
                             xn_w + self.init_w, s_ws, s_mmp)
        xi_w = [(s, s.n) for s in s_sts]
        if self.dev_stop == "l1win":
            s_cp = kb.sem()
            kb.dma("sync", x_next[:, :], xi_h[0:1024, :].bitcast(F32), waits=xi_w, sem=s_cp)
            return s_cp

        cw_c, w1_ = self.pc_load("L1_conv_w_c", 64)
        cb_c, w2_ = self.pc_load("L1_conv_b_c", 16)
        cl_w = w1_ + w2_
        ring_mark = [(), ()]
        for do in range(16):
            for b in range(BPC):
                i2 = (do * 2 + b) % 2
                base = i2 * 24
                xpu = self.ph[:, base * 512:base * 512 + 2052]
                acc1 = self.ph[:, (base + 5) * 512:(base + 5) * 512 + 2048]
                acc2 = self.ph[:, (base + 9) * 512:(base + 9) * 512 + 2048]
                ut = self.ph[:, (base + 13) * 512:(base + 13) * 512 + 2048]
                g = list(ring_mark[i2])
                kb.dma("scalar", xpu[:, 0:3], WD["c_zeros"][:, 0:3].bitcast(F32R),
                       waits=xi_w + g, sem=s_ldc[i2])
                kb.dma("scalar", xpu[:, 3:2051], xi_h[do * 128:(do + 1) * 128, b * SEQ:(b + 1) * SEQ],
                       waits=xi_w + g, sem=s_ldc[i2])
                ldv = (s_ldc[i2], s_ldc[i2].n)
                xpf = xpu.bitcast(F32)
                wc = lambda k, do=do: cw_c[:, do * 4 + k:do * 4 + k + 1]
                kb.ts("vector", acc1, xpf[:, 0:2048], wc(0), None, ALU.mult,
                      waits=cl_w + g + [ldv], incs=[(s_dv, 1)])
                kb.stt("vector", acc2, xpf[:, 1:2049], wc(1), acc1, ALU.mult, ALU.add,
                       waits=[(s_dv, s_dv.n)], incs=[(s_dv, 1)])
                kb.stt("vector", acc1, xpf[:, 2:2050], wc(2), acc2, ALU.mult, ALU.add,
                       waits=[(s_dv, s_dv.n)], incs=[(s_dv, 1)])
                kb.stt("vector", acc2, xpf[:, 3:2051], wc(3), acc1, ALU.mult, ALU.add,
                       waits=[(s_dv, s_dv.n)], incs=[(s_dv, 1)])
                kb.act(ut, acc2, AF.Silu, bias=cb_c[:, do:do + 1], waits=[(s_dv, s_dv.n)], incs=[(s_ev2, 1)])
                kb.dma("gpsimd", u_h[do * 128:(do + 1) * 128, b * SEQ:(b + 1) * SEQ], ut,
                       waits=[(s_ev2, s_ev2.n)], sem=s_stc[i2])
                ring_mark[i2] = [(s_stc[i2], s_stc[i2].n)]
        u_w = [(s_stc[0], s_stc[0].n), (s_stc[1], s_stc[1].n)]
        if self.dev_stop == "l1conv":
            s_cp = kb.sem()
            kb.dma("sync", x_next[:, :], u_h[0:1024, :].bitcast(F32), waits=u_w, sem=s_cp)
            return s_cp

        dr_sb = self.pbr[0:64, 0:4096]
        wxs = self.pw[:, 0:1536]
        kb.dma("sync", wxs, WD["L1_Wx_s"][:, :].bitcast(F32R), waits=[(s_mmp, s_mmp.n)], sem=s_wx)
        wxv = s_wx.n
        ut_ring = [self.hu(48 + i) for i in range(4)]
        ur_mark = [()] * 4
        for tch in range(8):
            ps_d = psb[0][0:64, 0:512]
            ps_b = psb[1][0:16, 0:512]
            ps_c = psb[2][0:16, 0:512]
            for k in range(16):
                rr = (tch * 16 + k) % 4
                kb.dma("scalar", ut_ring[rr], u_h[k * 128:(k + 1) * 128, tch * 512:(tch + 1) * 512],
                       waits=u_w + list(ur_mark[rr]), sem=s_ldr[rr])
                ldv = (s_ldr[rr], s_ldr[rr].n)
                for ps_, c0, c1, bk in ((ps_d, 0, 64, 0), (ps_b, 64, 80, 1), (ps_c, 80, 96, 2)):
                    kb.mm(ps_, wxs[:, k * 96 + c0:k * 96 + c1], ut_ring[rr], k == 0, k == 15,
                          waits=[(s_wx, wxv), ldv] + self.bw(bk),
                          incs=[(s_mmp, 1)] if bk == 2 else ())
                ur_mark[rr] = [(s_mmp, s_mmp.n)]
            mmv = s_mmp.n
            kb.copy("vector", dr_sb[:, tch * 512:(tch + 1) * 512], ps_d, waits=[(s_mmp, mmv)], incs=[(s_dv, 1)])
            kb.copy("vector", self.pbr[0:16, 4096 + tch * 512:4096 + (tch + 1) * 512], ps_b,
                    waits=[(s_mmp, mmv)], incs=[(s_dv, 1)])
            kb.copy("vector", self.pbr[64:80, 4096 + tch * 512:4096 + (tch + 1) * 512], ps_c,
                    waits=[(s_mmp, mmv)], incs=[(s_dv, 1)])
            for bk in (0, 1, 2):
                self.bmark[bk] = [(s_dv, s_dv.n)]
        xdbl_done = [(s_dv, s_dv.n)]
        if self.dev_stop == "l1xdbl":
            s_cp = kb.sem()
            kb.dma("sync", x_next[0:64, 0:4096].bitcast(F32R), dr_sb, waits=xdbl_done, sem=s_cp)
            kb.dma("sync", x_next[64:80, 0:4096].bitcast(F32R), self.pbr[0:16, 4096:8192], waits=xdbl_done, sem=s_cp)
            kb.dma("sync", x_next[80:96, 0:4096].bitcast(F32R), self.pbr[64:80, 4096:8192], waits=xdbl_done, sem=s_cp)
            return s_cp

        bdt, cl2 = self.pc_load("L1_b_dt_c", 16)
        wdts = self.pw[0:64, 1536:1536 + 2048]
        kb.dma("sync", wdts, WD["L1_Wdt_s"][:, :].bitcast(F32R),
               waits=[(s_mmp, s_mmp.n), (s_wx, s_wx.n)], sem=s_wd)
        wdtv = s_wd.n
        aa = self.pbf[:, 0:512]; ee = self.pbf[:, 512:1024]; ll = self.pbf[:, 1024:1536]
        rmx = self.pbf[:, 1536:2048]; dl = self.pbf[:, 2048:2560]
        import os as _os
        _ndd = int(_os.environ.get("KDD", "16"))
        for dd in range(_ndd):
            for tch in range(8):
                bk = 3 + (dd * 8 + tch) % 2
                ps = psb[bk][:, 0:512]
                kb.mm(ps, wdts[:, dd * 128:(dd + 1) * 128], dr_sb[:, tch * 512:(tch + 1) * 512],
                      True, True, waits=[(s_wd, wdtv)] + xdbl_done + self.bw(bk), incs=[(s_pe, 1)])
                pev = s_pe.n
                gd = []
                if _os.environ.get("KDSIMPLE"):
                    kb.copy("vector", dl, ps, waits=[(s_pe, pev)] + gd, incs=[(s_dv, 1)])
                    self.bmark[bk] = [(s_dv, s_dv.n)]
                    kb.dma("gpsimd", delta_h[dd * 128:(dd + 1) * 128, tch * 512:(tch + 1) * 512], dl,
                           waits=[(s_dv, s_dv.n)], sem=s_st)
                    continue
                zt = self.pbf[:, 2560 + 512:2560 + 1024] if False else aa
                kb.ts("vector", zt, ps, bdt[:, dd:dd + 1], None, ALU.add,
                      waits=cl2 + [(s_pe, pev)] + gd, incs=[(s_dv, 1)])
                kb.ts("vector", rmx, zt, 0.0, None, ALU.max,
                      waits=[(s_dv, s_dv.n)], incs=[(s_dv, 1)])
                kb.stt("vector", ee, rmx, 2.0, zt, ALU.mult, ALU.subtract,
                       waits=[(s_dv, s_dv.n)], incs=[(s_dv, 1)])
                abs_v = s_dv.n
                kb.act(ee, ee, AF.Exp, scale=-1.0, waits=[(s_dv, abs_v)], incs=[(s_ac, 1)])
                kb.act(ll, ee, AF.Ln, bias=self.epsc[:, 4:5], waits=[(s_ac, s_ac.n)], incs=[(s_ac, 1)])
                self.bmark[bk] = [(s_dv, s_dv.n)]
                kb.tt("vector", dl, ll, rmx, ALU.add,
                      waits=[(s_ac, s_ac.n), (s_std, s_std.n)],
                      incs=[(s_dv, 1)])
                kb.dma("gpsimd", delta_h[dd * 128:(dd + 1) * 128, tch * 512:(tch + 1) * 512], dl,
                       waits=[(s_dv, s_dv.n)], sem=s_std)
        delta_done = [(s_std, s_std.n)]
        if self.dev_stop == "l1delta":
            s_cp = kb.sem()
            kb.dma("sync", x_next[:, :], delta_h[0:1024, :], waits=delta_done, sem=s_cp)
            return s_cp

        qa = [self.pbf[:, 2560 + i * 128:2560 + (i + 1) * 128] for i in range(2)]
        sa = [self.pbf[:, 2816 + i * 128:2816 + (i + 1) * 128] for i in range(2)]
        slabs = [self.pbr[:, 1024 + i * 1024:1024 + (i + 1) * 1024] for i in range(2)]
        dq_mark = [(), ()]
        kt_mark = [(), ()]
        for i in range(6):
            for do in range(DB):
                r2 = (i * 8 + do) % 2
                for kblk in range(DB):
                    rk = kblk % 2
                    kb.dma("scalar", qa[rk].bitcast(I32), WD["L1_qT_s"][i, kblk, :, do * 128:(do + 1) * 128],
                           waits=xdbl_done + list(kt_mark[rk]) + list(dq_mark[r2]), sem=s_ldq[rk])
                    kb.dma("scalar", sa[rk], WD["L1_srepT_s"][i, kblk, :, do * 128:(do + 1) * 128],
                           sem=s_ldq[rk])
                    ldv = (s_ldq[rk], s_ldq[rk].n)
                    kb.ts("vector", qa[rk], qa[rk].bitcast(I32), 1.0 / 7.5, 1.0,
                          ALU.mult, ALU.subtract, waits=[ldv] + list(dq_mark[r2]), incs=[(s_dv, 1)])
                    kb.tt("vector", slabs[r2][:, kblk * 128:(kblk + 1) * 128], qa[rk], sa[rk],
                          ALU.mult, waits=[(s_dv, s_dv.n)], incs=[(s_dv, 1)])
                    kt_mark[rk] = [(s_dv, s_dv.n)]
                kb.dma("gpsimd", wdq_h[i, do], slabs[r2], waits=[(s_dv, s_dv.n)], sem=s_stq[r2])
                dq_mark[r2] = [(s_stq[r2], s_stq[r2].n)]
        dq_done = [(s_stq[0], s_stq[0].n), (s_stq[1], s_stq[1].n)]

        self.qlora_ffn(xn_h, x_cur, xm_h, dq_done + delta_done, wdq_h)

        if self.dev_stop == "l1qlora":
            s_cp = kb.sem()
            kb.dma("sync", x_next[:, :], xm_h[:, :], waits=[(self.s_ql, self.s_ql.n)], sem=s_cp)
            return s_cp
        self.scan_phase(u_h, delta_h, sres_h, ymix_h, [(self.s_ql, self.s_ql.n)])

        if self.dev_stop == "l1scan":
            s_cp = kb.sem()
            kb.dma("sync", x_next[:, :], ymix_h[0:1024, :].bitcast(F32),
                   waits=[(self.s_scan, self.s_scan.n)], sem=s_cp)
            return s_cp
        return self.wout_proj(ymix_h, xm_h, x_next, [(self.s_scan, self.s_scan.n)])

    def qlora_ffn(self, xn_h, x_in, x_out, entry, wdq_h):
        kb, psb, WD = self.kb, self.psb, self.WD
        s_x = self.S("q_x"); s_pe = self.S("a_pe"); s_dv = self.S("a_dv"); s_ac = self.S("a_ac")
        s_wl = self.S("q_wl"); s_wq = self.SL("w_s", 4)
        s_mm = self.S("w_mm"); s_ev = self.S("q_ev"); s_ge = self.S("q_ge")
        s_st = self.S("q_st")
        bcol, wb_ = self.pc_load("L1_b_c", 48)
        gcol, wg_ = self.pc_load("L1_lng_c", 24)
        bbcol, wbb_ = self.pc_load("L1_lnb_c", 24)
        cl_w = wb_ + wg_ + wbb_
        h_t = [self.ph[:, k * 1024:(k + 1) * 1024] for k in range(DB)]
        y_t = [self.ph[:, 8192 + k * 1024:8192 + (k + 1) * 1024] for k in range(DB)]
        g_t = [self.ph[:, 16384 + k * 1024:16384 + (k + 1) * 1024] for k in range(DB)]
        sq_t = [self.ph[:, 24576 + k * 1024:24576 + (k + 1) * 1024] for k in range(DB)]
        u1_sb = self.pbr[0:32, 0:1024]
        la_sb = self.pbr[:, 3072:3328]
        lb_sb = self.pbr[0:32, 1024:2048]
        mrow = self.ps_[0:1, 0:1024]; qrow = self.ps_[0:1, 1024:2048]
        vrow = self.pbf[0:1, 1024:2048]; lrow = self.pbf[0:1, 2048:3072]
        nmean_r = self.pbr[0:1, 8448:8960]
        rstd_r = self.pbr[0:1, 8960:9472]
        row_mark = []
        TS = 1024
        prev_sc = []
        for sc in range(4):
            for d in range(DB):
                kb.dma("scalar", h_t[d], xn_h[d * 128:(d + 1) * 128, sc * TS:(sc + 1) * TS],
                       waits=entry + prev_sc, sem=s_x)
            xw = [(s_x, s_x.n)]
            import os as _os
            for blk in range(int(_os.environ.get('KQBLK', '3'))):
                for lin in range(2):
                    i = 2 * blk + lin
                    src_t = h_t if lin == 0 else g_t
                    dst_t = y_t
                    if blk == 0 and lin == 0:
                        src_w = xw
                    elif lin == 1:
                        src_w = [(s_ge, s_ge.n)]
                    else:
                        src_w = [(s_ev, s_ev.n)]
                    kb.dma("sync", la_sb, WD["L1_la_s"][i].bitcast(F32R),
                           waits=entry + [(s_mm, s_mm.n)], sem=s_wl)
                    kb.dma("sync", lb_sb, WD["L1_lbT_s"][i].bitcast(F32R), sem=s_wl)
                    lav = s_wl.n
                    for th in range(2):
                        psu = psb[6][0:32, 0:512]
                        for k in range(DB):
                            kb.mm(psu, la_sb[:, k * 32:(k + 1) * 32],
                                  src_t[k][:, th * 512:(th + 1) * 512], k == 0, k == DB - 1,
                                  waits=entry + src_w + self.bw(6) + [(s_wl, lav)],
                                  incs=[(s_mm, 1)] if k == DB - 1 else ())
                        kb.copy("vector", u1_sb[:, th * 512:(th + 1) * 512], psu,
                                waits=[(s_mm, s_mm.n)], incs=[(s_dv, 1)])
                        self.bmark[6] = [(s_dv, s_dv.n)]
                    u1_w = [(s_dv, s_dv.n)]
                    wbuf = [self.pw[:, r * 1024:(r + 1) * 1024] for r in range(4)]
                    wfree = [()] * 4
                    for do in range(DB):
                        r4 = do % 4
                        kb.dma("sync", wbuf[r4], wdq_h[i, do],
                               waits=entry + list(wfree[r4]) + [(s_wl, lav)], sem=s_wq[r4])
                        wv = (s_wq[r4], s_wq[r4].n)
                        for th in range(2):
                            bk = (do * 2 + th) % 4
                            ps = psb[bk][:, 0:512]
                            for k in range(DB):
                                kb.mm(ps, wbuf[r4][:, k * 128:(k + 1) * 128],
                                      src_t[k][:, th * 512:(th + 1) * 512], k == 0, False,
                                      waits=src_w + self.bw(bk) + [wv] + u1_w)
                            kb.mm(ps, lb_sb[:, do * 128:(do + 1) * 128],
                                  u1_sb[:, th * 512:(th + 1) * 512], False, True,
                                  waits=u1_w, incs=[(s_mm, 1)])
                            kb.ts("vector", dst_t[do][:, th * 512:(th + 1) * 512], ps,
                                  bcol[:, i * 8 + do:i * 8 + do + 1], None, ALU.add,
                                  waits=cl_w + [(s_mm, s_mm.n)], incs=[(s_ev, 1)])
                            self.bmark[bk] = [(s_ev, s_ev.n)]
                        wfree[r4] = [(s_mm, s_mm.n)]
                    y_w = [(s_ev, s_ev.n)]
                    if lin == 0:
                        for d in range(DB):
                            kb.act(sq_t[d], y_t[d], AF.Square, waits=y_w, incs=[(s_ac, 1)])
                        sq_w = [(s_ac, s_ac.n)]
                        for th in range(2):
                            psS = psb[4][0:1, 0:512]
                            psQ = psb[5][0:1, 0:512]
                            for d in range(DB):
                                kb.mm(psS, self.onec_r, y_t[d][:, th * 512:(th + 1) * 512],
                                      d == 0, d == DB - 1, waits=y_w + self.bw(4) + self.init_w,
                                      incs=[(s_mm, 1)] if d == DB - 1 else ())
                            for d in range(DB):
                                kb.mm(psQ, self.onec_r, sq_t[d][:, th * 512:(th + 1) * 512],
                                      d == 0, d == DB - 1, waits=sq_w + self.bw(5),
                                      incs=[(s_mm, 1)] if d == DB - 1 else ())
                            mmv = s_mm.n
                            mh = mrow[0:1, th * 512:(th + 1) * 512]
                            qh = qrow[0:1, th * 512:(th + 1) * 512]
                            vh = vrow[0:1, th * 512:(th + 1) * 512]
                            lh = lrow[0:1, th * 512:(th + 1) * 512]
                            kb.ts("vector", mh, psS, 1.0 / D, None, ALU.mult, waits=[(s_mm, mmv - 1)], incs=[(s_dv, 1)])
                            kb.ts("vector", qh, psQ, 1.0 / D, None, ALU.mult, waits=[(s_mm, mmv)], incs=[(s_dv, 1)])
                            self.bmark[4] = [(s_dv, s_dv.n - 1)]
                            self.bmark[5] = [(s_dv, s_dv.n)]
                            kb.tt("vector", vh, mh, mh, ALU.mult,
                                  waits=[(s_dv, s_dv.n), (s_ac, s_ac.n)], incs=[(s_dv, 1)])
                            kb.tt("vector", vh, qh, vh, ALU.subtract, waits=[(s_dv, s_dv.n)], incs=[(s_dv, 1)])
                            kb.ts("vector", nmean_r[0:1, 0:512], mh, -1.0, None, ALU.mult,
                                  waits=row_mark, incs=[(s_dv, 1)])
                            kb.act(lh, vh, AF.Ln, bias=self.epsc[0:1, 0:1], waits=[(s_dv, s_dv.n - 1)] + self.init_w, incs=[(s_ac, 1)])
                            kb.act(rstd_r[0:1, 0:512], lh, AF.Exp, scale=-0.5,
                                   waits=[(s_ac, s_ac.n)] + row_mark, incs=[(s_ac, 1)])
                            psMB = psb[4][:, 0:512]
                            psRB = psb[5][:, 0:512]
                            kb.mm(psMB, self.ones_r, nmean_r[0:1, 0:512], True, True,
                                  waits=self.bw(4) + [(s_dv, s_dv.n)], incs=[(s_mm, 1)])
                            kb.mm(psRB, self.ones_r, rstd_r[0:1, 0:512], True, True,
                                  waits=self.bw(5) + [(s_ac, s_ac.n)], incs=[(s_mm, 1)])
                            bcv = s_mm.n
                            row_mark = [(s_mm, bcv)]
                            dv0 = s_dv.n
                            for d in range(DB):
                                yy = y_t[d][:, th * 512:(th + 1) * 512]
                                gg = g_t[d][:, th * 512:(th + 1) * 512]
                                kb.tt("vector", gg, yy, psMB, ALU.add, waits=[(s_mm, bcv)], incs=[(s_dv, 1)])
                            for d in range(DB):
                                gg = g_t[d][:, th * 512:(th + 1) * 512]
                                kb.tt("vector", gg, gg, psRB, ALU.mult,
                                      waits=[(s_dv, dv0 + d + 1)], incs=[(s_dv, 1)])
                            dv1 = s_dv.n
                            for d in range(DB):
                                gg = g_t[d][:, th * 512:(th + 1) * 512]
                                kb.ts("vector", gg, gg, gcol[:, blk * 8 + d:blk * 8 + d + 1],
                                      bbcol[:, blk * 8 + d:blk * 8 + d + 1], ALU.mult, ALU.add,
                                      waits=[(s_dv, dv1 - DB + d + 1)], incs=[(s_dv, 1)])
                            self.bmark[4] = [(s_dv, s_dv.n)]
                            self.bmark[5] = [(s_dv, s_dv.n)]
                        gelu_w = [(s_dv, s_dv.n)]
                        for d in range(DB):
                            kb.act(g_t[d], g_t[d], AF.Gelu, waits=gelu_w, incs=[(s_ge, 1)])
                    else:
                        for d in range(DB):
                            kb.tt("vector", h_t[d], h_t[d], y_t[d], ALU.add,
                                  waits=y_w, incs=[(s_ev, 1)])
            xt = self.pbf[:, 0:1024]
            for d in range(DB):
                kb.dma("scalar", xt, x_in[d * 128:(d + 1) * 128, sc * TS:(sc + 1) * TS],
                       waits=[(s_ev, s_ev.n), (s_st, s_st.n)], sem=s_x)
                kb.tt("vector", g_t[d], h_t[d], xt, ALU.add,
                      waits=[(s_x, s_x.n), (s_ev, s_ev.n)], incs=[(s_ev, 1)])
                kb.dma("gpsimd", x_out[d * 128:(d + 1) * 128, sc * TS:(sc + 1) * TS].bitcast(F32R),
                       g_t[d], waits=[(s_ev, s_ev.n)], sem=s_st)
            prev_sc = [(s_st, s_st.n), (s_mm, s_mm.n)]
        self.s_ql = s_st
        return s_st

    def scan_phase(self, u_h, delta_h, sres_h, ymix_h, entry):
        kb, psb, WD = self.kb, self.psb, self.WD
        s_tile = self.S("sc_tile"); s_rb = self.S("sc_rb"); s_rc = self.S("sc_rc")
        s_pe = self.S("a_pe"); s_dv = self.S("a_dv"); s_ac = self.S("a_ac")
        s_st = self.S("sc_st"); s_row = self.S("sc_row")
        al_sb = self.ps_[:, 0:256]
        e_sb = self.ps_[:, 256:512]
        d_sb = self.ps_[:, 512:528]
        s_cl = kb.sem()
        kb.dma("sync", al_sb, WD["L1_Alog_c"][:, :], waits=entry, sem=s_cl)
        kb.dma("sync", d_sb, WD["L1_D_c"][:, :], sem=s_cl)
        kb.act(e_sb, al_sb, AF.Exp, waits=[(s_cl, s_cl.n)], incs=[(s_ac, 1)])
        e_w = [(s_ac, s_ac.n)]
        U = lambda u0, n=4: self.ph[:, u0 * 512:(u0 + n) * 512]
        ut = U(0); du = U(4); ya = U(12)
        sr = U(16); dt = U(20)
        cd = self.pbf[:, 0:2048]
        dAc = [U(24), U(44)]
        zz = [U(28), U(48)]
        sin = [U(32), U(52)]
        dud = [U(36), U(56)]
        num = [U(40), U(60)]
        rowB = self.pbr[0:1, 0:2048]
        rowC = self.pbr[0:1, 2048:4096]
        bcol = self.ps_[:, 528:530]
        ones_bc = self.epsc[:, 4:5].broadcast_to([128, 2048])
        prev_done = []
        for b in range(BPC):
            for dblk in range(16):
                c0 = b * SEQ
                kb.dma("sync", dt, delta_h[dblk * 128:(dblk + 1) * 128, c0:c0 + SEQ].bitcast(F32R),
                       waits=entry + prev_done, sem=s_tile)
                kb.dma("sync", ut, u_h[dblk * 128:(dblk + 1) * 128, c0:c0 + SEQ],
                       waits=entry + prev_done, sem=s_tile)
                kb.dma("sync", sr, sres_h[dblk * 128:(dblk + 1) * 128, c0:c0 + SEQ].bitcast(F32R),
                       sem=s_tile)
                ldv = s_tile.n
                dtf = dt.bitcast(F32)
                kb.tt("vector", du, dtf, ut, ALU.mult, waits=[(s_tile, ldv)] + prev_done, incs=[(s_dv, 1)])
                kb.scan(cd, ones_bc, dtf, 0.0, ALU.mult, ALU.add,
                        waits=[(s_tile, ldv)] + self.init_w + prev_done, incs=[(s_dv, 1)])
                cd_w = [(s_dv, s_dv.n)]
                nm = [list(cd_w), list(cd_w)]
                for n in range(16):
                    rn = n % 2
                    ecol = e_sb[:, dblk * 16 + n:dblk * 16 + n + 1]
                    g = list(nm[rn])
                    kb.ts("vector", bcol[:, rn:rn + 1], cd[:, 2047:2048], ecol, -1.0,
                          ALU.mult, ALU.mult, waits=cd_w + e_w + g + [(s_ac, s_ac.n)], incs=[(s_dv, 1)])
                    kb.ts("vector", zz[rn], cd, ecol, bcol[:, rn:rn + 1], ALU.mult, ALU.add,
                          waits=[(s_dv, s_dv.n)], incs=[(s_dv, 1)])
                    kb.act(dAc[rn], zz[rn], AF.Exp, waits=[(s_dv, s_dv.n)], incs=[(s_ac, 1)])
                    dac_w = [(s_ac, s_ac.n)]
                    kb.tt("vector", dud[rn], du, dAc[rn], ALU.mult, waits=dac_w, incs=[(s_dv, 1)])
                    dud_v = s_dv.n
                    kb.dma("sync", rowB, self.pbr[n:n + 1, 4096 + c0:4096 + c0 + SEQ],
                           waits=[(s_row, s_row.n)] + g, sem=s_rb)
                    rbv = s_rb.n
                    for q in range(4):
                        pb = psb[rn][:, 0:512]
                        kb.mm(pb, self.ones_r, rowB[:, q * 512:(q + 1) * 512], True, True,
                              waits=[(s_rb, rbv)] + self.bw(rn) + self.init_w, incs=[(s_pe, 1)])
                        kb.tt("vector", sin[rn][:, q * 512:(q + 1) * 512],
                              dud[rn][:, q * 512:(q + 1) * 512], pb, ALU.mult,
                              waits=[(s_pe, s_pe.n), (s_dv, s_dv.n)], incs=[(s_dv, 1)])
                        self.bmark[rn] = [(s_dv, s_dv.n)]
                    kb.scan(num[rn], ones_bc, sin[rn], 0.0, ALU.mult, ALU.add,
                            waits=[(s_dv, s_dv.n)], incs=[(s_dv, 1)])
                    num_w = [(s_dv, s_dv.n)]
                    kb.act(zz[rn], dAc[rn], AF.Ln, bias=self.epsc[:, 2:3], waits=dac_w, incs=[(s_ac, 1)])
                    kb.act(dAc[rn], zz[rn], AF.Exp, scale=-1.0,
                           waits=[(s_ac, s_ac.n), (s_dv, s_dv.n)], incs=[(s_ac, 1)])
                    rec_w = [(s_ac, s_ac.n)]
                    kb.dma("sync", rowC, self.pbr[64 + n:64 + n + 1, 4096 + c0:4096 + c0 + SEQ],
                           waits=[(s_row, s_row.n)] + g, sem=s_rc)
                    rcv = s_rc.n
                    for q in range(4):
                        pb = psb[2 + rn][:, 0:512]
                        kb.mm(pb, self.ones_r, rowC[:, q * 512:(q + 1) * 512], True, True,
                              waits=[(s_rc, rcv)] + self.bw(2 + rn), incs=[(s_pe, 1)])
                        kb.tt("vector", dud[rn][:, q * 512:(q + 1) * 512],
                              dAc[rn][:, q * 512:(q + 1) * 512], pb, ALU.mult,
                              waits=[(s_pe, s_pe.n)] + rec_w, incs=[(s_dv, 1)])
                        self.bmark[2 + rn] = [(s_dv, s_dv.n)]
                    kb.emit("vector", lambda e: e.engine_nop(), waits=[(s_dv, s_dv.n)], incs=[(s_row, 1)])
                    kb.tt("vector", zz[rn], num[rn], dud[rn], ALU.mult,
                          waits=num_w + [(s_dv, s_dv.n)], incs=[(s_dv, 1)])
                    if n == 0:
                        kb.copy("vector", ya, zz[rn], waits=[(s_dv, s_dv.n)], incs=[(s_dv, 1)])
                    else:
                        kb.tt("vector", ya, ya, zz[rn], ALU.add, waits=[(s_dv, s_dv.n)], incs=[(s_dv, 1)])
                    nm[rn] = [(s_dv, s_dv.n)]
                dcol = d_sb[:, dblk:dblk + 1]
                kb.stt("vector", du, ut, dcol, ya, ALU.mult, ALU.add,
                       waits=[(s_dv, s_dv.n), (s_tile, ldv)] + [(s_cl, s_cl.n)], incs=[(s_dv, 1)])
                kb.tt("vector", dud[0], du, sr.bitcast(F32), ALU.mult, waits=[(s_dv, s_dv.n)], incs=[(s_dv, 1)])
                kb.dma("gpsimd", ymix_h[dblk * 128:(dblk + 1) * 128, c0:c0 + SEQ], dud[0],
                       waits=[(s_dv, s_dv.n)], sem=s_st)
                prev_done = [(s_st, s_st.n)]
        self.s_scan = s_st
        return s_st

    def wout_proj(self, ymix_h, xm_h, x_next, entry):
        kb, psb, WD = self.kb, self.psb, self.WD
        s_ym = self.SL("wo_ym", 2); s_xt = self.SL("wo_xt", 2)
        s_pe = self.S("a_pe"); s_dv = self.S("a_dv"); s_sto = self.SL("wo_st", 2)
        s_wo = self.SL("f_w2", 2); s_g = self.S("wo_g")
        bout, cl_w = self.pc_load("L1_bout_c", 8)
        wbuf = [self.pw[:, i * 2048:(i + 1) * 2048] for i in range(2)]
        wfree = [(), ()]
        ot = [self.pbf[:, i * 512:(i + 1) * 512] for i in range(2)]
        xt = [self.pbf[:, 1024 + i * 512:1024 + (i + 1) * 512] for i in range(2)]
        for tch in range(8):
            r = tch % 2
            ym = [self.ph[:, (r * 16 + k) * 512:(r * 16 + k + 1) * 512] for k in range(16)]
            g = [(s_pe, s_pe.n)] if tch >= 2 else []
            for k in range(16):
                kb.dma("scalar", ym[k], ymix_h[k * 128:(k + 1) * 128, tch * 512:(tch + 1) * 512],
                       waits=entry + g, sem=s_ym[r])
            ym_w = [(s_ym[r], s_ym[r].n)]
            for do in range(DB):
                rw = do % 2
                kb.dma("sync", wbuf[rw], WD["L1_Wout_t"][do].bitcast(F32R),
                       waits=entry + list(wfree[rw]), sem=s_wo[rw])
                wv = (s_wo[rw], s_wo[rw].n)
                i2 = do % 2
                kb.dma("scalar", xt[i2], xm_h[do * 128:(do + 1) * 128, tch * 512:(tch + 1) * 512],
                       waits=entry + [(s_sto[i2], s_sto[i2].n)], sem=s_xt[i2])
                xv = (s_xt[i2], s_xt[i2].n)
                bk = 4 + do % 2
                ps = psb[bk][:, 0:512]
                for k in range(16):
                    kb.mm(ps, wbuf[rw][:, k * 128:(k + 1) * 128], ym[k], k == 0, k == 15,
                          waits=ym_w + [wv] + self.bw(bk),
                          incs=[(s_pe, 1)] if k == 15 else ())
                wfree[rw] = [(s_pe, s_pe.n)]
                kb.ts("vector", ot[i2], ps, bout[:, do:do + 1], None, ALU.add,
                      waits=cl_w + [(s_pe, s_pe.n), (s_sto[i2], s_sto[i2].n)],
                      incs=[(s_dv, 1)])
                self.bmark[bk] = [(s_dv, s_dv.n)]
                kb.tt("gpsimd", ot[i2], ot[i2], xt[i2], ALU.add,
                      waits=[xv, (s_dv, s_dv.n)], incs=[(s_g, 1)])
                kb.dma("gpsimd", x_next[do * 128:(do + 1) * 128, tch * 512:(tch + 1) * 512],
                       ot[i2], waits=[(s_g, s_g.n)], sem=s_sto[i2])
        s_done = kb.sem()
        kb.emit("gpsimd", lambda e: e.engine_nop(),
                waits=[(s_sto[0], s_sto[0].n), (s_sto[1], s_sto[1].n)], incs=[(s_done, 1)])
        return s_done

    def moe_ffn(self, xn_h, x_in, x_out, entry, pfx, use_rpm, gelu_exact):
        kb, psb, WD = self.kb, self.psb, self.WD
        s_pe = self.S("a_pe"); s_dv = self.S("a_dv"); s_ac = self.S("a_ac")
        s_mm = self.S("w_mm")
        s_xl = self.SL("ld_c", 2); s_sel = self.S("a_vt"); s_selst = self.S("q_st")
        s_ost = [self.S("st_s2"), self.S("st_s3")]
        s_xst = [self.S("a_stm0"), self.S("a_stm1")]
        s_wA = self.SL("w_s", 4); s_wB = [self.S("st_s4"), self.S("st_s5")]
        s_rowd = self.S("sc_rc")
        s_cld = self.S("q_wl")
        sel0 = s_selst.n
        selT_h = self.dram(pfx + "selT", [8, T], F32R)
        swb, w_swb = self.pc_load(pfx + "swb_rep", 8)
        b1c = self.ps_[:, 1024:1280]
        kb.dma("sync", b1c, WD[pfx + "b1_c"][:, :], waits=entry, sem=s_cld)
        sww = self.pw[0:128, 0:64]
        kb.dma("sync", sww, WD[pfx + "sw_s"].bitcast(F32R), waits=entry, sem=s_cld)
        b2l = self.pbr[0:8, 3072:4096]
        kb.dma("sync", b2l, WD[pfx + "b2_lhs"].bitcast(F32R), waits=entry, sem=s_cld)
        xn_t = [self.ph[:, d * T:(d + 1) * T] for d in range(DB)]
        for d in range(DB):
            kb.dma("sync", xn_t[d], xn_h[d * 128:(d + 1) * 128, :], waits=entry, sem=s_cld)
        cld_w = [(s_cld, s_cld.n)]
        w_b1 = cld_w
        xn_w = cld_w
        gelu_f = AF.Gelu if gelu_exact else AF.Gelu_apprx_tanh
        lgt = self.pbf[:, 0:8]
        mx = self.pbf[:, 8:9]; nt = self.pbf[:, 9:10]; eq = self.pbf[:, 16:24]
        erow = self.pbf[:, 24:25]; srow = self.pbf[:, 32:33]; rrow = self.pbf[:, 33:34]
        sel = self.pbr[:, 4608:4616]
        for tb in range(32):
            ps = psb[6][:, 0:8]
            for k in range(DB):
                kb.mm(ps, xn_t[k][:, tb * 128:(tb + 1) * 128],
                      sww[:, k * 8:(k + 1) * 8], k == 0, k == DB - 1,
                      waits=xn_w + self.bw(6),
                      incs=[(s_mm, 1)] if k == DB - 1 else ())
            mmv = s_mm.n
            kb.tt("vector", lgt, ps, swb, ALU.add,
                  waits=[(s_mm, mmv)] + w_swb + [(s_dv, s_dv.n), (s_ac, s_ac.n)],
                  incs=[(s_dv, 1)])
            self.bmark[6] = [(s_dv, s_dv.n)]
            kb.red("vector", mx, lgt, ALU.max, waits=[(s_dv, s_dv.n)], incs=[(s_dv, 1)])
            kb.tt("vector", eq, lgt, mx.broadcast_to([128, 8]), ALU.is_ge,
                  waits=[(s_dv, s_dv.n), (s_ac, s_ac.n)], incs=[(s_dv, 1)])
            kb.ts("vector", nt, mx, 0.0, 1.0, ALU.mult, ALU.add,
                  waits=[(s_dv, s_dv.n)], incs=[(s_dv, 1)])
            for e in range(8):
                kb.tt("vector", sel[:, e:e + 1], eq[:, e:e + 1], nt, ALU.mult,
                      waits=[(s_dv, s_dv.n), (s_sel, s_sel.n)], incs=[(s_dv, 1)])
                kb.stt("vector", nt, sel[:, e:e + 1], -1.0, nt, ALU.mult, ALU.add,
                       waits=[(s_dv, s_dv.n)], incs=[(s_dv, 1)])
            if use_rpm:
                kb.ts("vector", erow, mx, -1.0, None, ALU.mult,
                      waits=[(s_dv, s_dv.n), (s_ac, s_ac.n)], incs=[(s_dv, 1)])
                kb.act(eq, lgt, AF.Exp, bias=erow[:, 0:1],
                       waits=[(s_dv, s_dv.n)], incs=[(s_ac, 1)])
                kb.red("vector", srow, eq, ALU.add, waits=[(s_ac, s_ac.n)], incs=[(s_dv, 1)])
                kb.recip(rrow, srow, waits=[(s_dv, s_dv.n)], incs=[(s_dv, 1)])
                kb.ts("vector", sel, sel, rrow, None, ALU.mult,
                      waits=[(s_dv, s_dv.n)], incs=[(s_dv, 1)])
            pt = psb[7][0:8, 0:128]
            kb.trans(pt, sel.bitcast(F32), self.ident,
                     waits=[(s_dv, s_dv.n)] + self.bw(7) + self.init_w, incs=[(s_pe, 1)])
            kb.copy("vector", self.pbr[0:8, 8192:8192 + 128], pt.bitcast(F32R),
                    waits=[(s_pe, s_pe.n), (s_selst, sel0 + 16 * tb)], incs=[(s_sel, 1)])
            self.bmark[7] = [(s_sel, s_sel.n)]
            kb.dma("gpsimd", selT_h[:, tb * 128:(tb + 1) * 128], self.pbr[0:8, 8192:8192 + 128],
                   waits=[(s_sel, s_sel.n)], sem=s_selst)
        sel_done = [(s_selst, s_selst.n)]

        seltile = [self.pbr[0:8, 2048 + i * 512:2048 + (i + 1) * 512] for i in range(2)]
        s_stl = [self.S("f_w20"), self.S("f_w21")]
        s_xli = self.SL("wo_xt", 2)
        xt = [self.pbf[:, i * 512:(i + 1) * 512] for i in range(2)]
        ot = [self.pbf[:, 1024 + i * 512:1024 + (i + 1) * 512] for i in range(2)]
        for tch in range(NT):
            i2 = tch % 2
            kb.dma("sync", seltile[i2], selT_h[:, tch * 512:(tch + 1) * 512],
                   waits=sel_done + [(s_mm, s_mm.n)], sem=s_stl[i2])
            stw = (s_stl[i2], s_stl[i2].n)
            for do2 in range(DB):
                j2 = do2 % 2
                kb.dma("scalar", xt[j2], x_in[do2 * 128:(do2 + 1) * 128, tch * 512:(tch + 1) * 512],
                       waits=entry + sel_done + [(s_xst[j2], s_xst[j2].n)], sem=s_xli[j2])
                xv = (s_xli[j2], s_xli[j2].n)
                bk = 4 + j2
                ps = psb[bk][:, 0:512]
                kb.mm(ps, b2l[:, do2 * 128:(do2 + 1) * 128], seltile[i2], True, True,
                      waits=[stw] + cld_w + self.bw(bk), incs=[(s_mm, 1)])
                kb.tt("vector", ot[j2], ps, xt[j2], ALU.add,
                      waits=[(s_mm, s_mm.n), xv, (s_xst[j2], s_xst[j2].n)], incs=[(s_dv, 1)])
                self.bmark[bk] = [(s_dv, s_dv.n)]
                kb.dma("gpsimd", x_out[do2 * 128:(do2 + 1) * 128, tch * 512:(tch + 1) * 512],
                       ot[j2], waits=[(s_dv, s_dv.n)], sem=s_xst[j2])
        xinit_done = [(s_xst[0], s_xst[0].n), (s_xst[1], s_xst[1].n)]

        w1sl = [self.ph[:, i * 1024:(i + 1) * 1024] for i in range(16)]
        w2sl = [self.ph[:, 16384 + i * 2048:16384 + (i + 1) * 2048] for i in range(8)]
        ht = [self.pbr[:, i * 256:(i + 1) * 256] for i in range(16)]
        xnt2 = [[self.pw[:, ip * 2048 + d * 256:ip * 2048 + (d + 1) * 256] for d in range(DB)]
                for ip in range(2)]
        maskb = [self.pbf[:, 2048 + ip * 256:2048 + (ip + 1) * 256] for ip in range(2)]
        osb = [self.pbf[:, 2560 + i * 256:2560 + (i + 1) * 256] for i in range(2)]
        rowsc = self.pbr[0:1, 8192:8448]
        NTB = 16
        xg = [(), ()]
        mg = [(), ()]
        for e in range(8):
            for fh in range(2):
                prevw = [(s_mm, s_mm.n)]
                baseA = [s.n for s in s_wA]
                for do1 in range(16):
                    kb.dma("sync", w1sl[do1], WD[pfx + "w1_t"][e, fh * 16 + do1].bitcast(F32R),
                           waits=prevw, sem=s_wA[do1 // 4])
                baseB = [s.n for s in s_wB]
                for do2 in range(DB):
                    kb.dma("gpsimd", w2sl[do2],
                           WD[pfx + "w2_t"][e, do2][:, fh * 2048:(fh + 1) * 2048].bitcast(F32R),
                           waits=prevw, sem=s_wB[do2 // 4])
                for tb in range(NTB):
                    ip = tb % 2
                    for d in range(DB):
                        kb.dma("scalar", xnt2[ip][d],
                               xn_h[d * 128:(d + 1) * 128, tb * 256:(tb + 1) * 256],
                               waits=entry + list(xg[ip]), sem=s_xl[ip])
                    xw = [(s_xl[ip], s_xl[ip].n)]
                    kb.dma("sync", rowsc, selT_h[e:e + 1, tb * 256:(tb + 1) * 256],
                           waits=sel_done + [(s_pe, s_pe.n)], sem=s_rowd)
                    psm = psb[3][:, 0:256]
                    kb.mm(psm, self.ones_r, rowsc, True, True,
                          waits=[(s_rowd, s_rowd.n)] + self.bw(3) + self.init_w, incs=[(s_pe, 1)])
                    kb.copy("vector", maskb[ip], psm,
                            waits=[(s_pe, s_pe.n), (s_dv, s_dv.n)] + list(mg[ip]), incs=[(s_dv, 1)])
                    self.bmark[3] = [(s_dv, s_dv.n)]
                    msk_w = [(s_dv, s_dv.n)]
                    for do1 in range(16):
                        bk = do1 % 3
                        ps = psb[bk][:, 0:256]
                        for k in range(DB):
                            kb.mm(ps, w1sl[do1][:, k * 128:(k + 1) * 128], xnt2[ip][k],
                                  k == 0, k == DB - 1,
                                  waits=xw + self.bw(bk) + [(s_wA[do1 // 4], baseA[do1 // 4] + 64)],
                                  incs=[(s_mm, 1)] if k == DB - 1 else ())
                        kb.act(ht[do1], ps, gelu_f,
                               bias=b1c[:, e * 32 + fh * 16 + do1:e * 32 + fh * 16 + do1 + 1],
                               waits=w_b1 + [(s_mm, s_mm.n)], incs=[(s_ac, 1)])
                        self.bmark[bk] = [(s_ac, s_ac.n)]
                    xg[ip] = [(s_mm, s_mm.n)]
                    hready = [(s_ac, s_ac.n)]
                    for do2 in range(DB):
                        bk = 4 + do2 % 4
                        ps = psb[bk][:, 0:256]
                        for k2 in range(16):
                            kb.mm(ps, w2sl[do2][:, k2 * 128:(k2 + 1) * 128], ht[k2],
                                  k2 == 0, k2 == 15,
                                  waits=hready + self.bw(bk) + [(s_wB[do2 // 4], baseB[do2 // 4] + 64)],
                                  incs=[(s_mm, 1)] if k2 == 15 else ())
                        i2 = do2 % 2
                        kb.tt("vector", osb[i2], ps, maskb[ip], ALU.mult,
                              waits=[(s_mm, s_mm.n), (s_ost[i2], s_ost[i2].n)] + msk_w + xinit_done,
                              incs=[(s_dv, 1)])
                        self.bmark[bk] = [(s_dv, s_dv.n)]
                        kb.emit("gpsimd",
                                lambda eng, o=x_out[do2 * 128:(do2 + 1) * 128,
                                                    tb * 256:(tb + 1) * 256], i=osb[i2]:
                                eng.dma_start(out=o, in_=i, accum_op=ALU.add),
                                waits=[(s_dv, s_dv.n)], incs=[(s_ost[i2], 16)])
                    mg[ip] = [(s_dv, s_dv.n)]
        self._last_pw_user = [(s_mm, s_mm.n)]
        s_done = kb.sem()
        kb.emit("gpsimd", lambda e2: e2.engine_nop(),
                waits=[(s_ost[0], s_ost[0].n), (s_ost[1], s_ost[1].n)], incs=[(s_done, 1)])
        return s_done

    def layer2(self, x_cur, x_next, entry=()):
        kb, psb, WD = self.kb, self.psb, self.WD
        entry = list(entry)
        xn_h = self.dram("xn2", [D, T], F32R)
        xm_h = self.dram("xm2", [D, T], F32)
        q_h = self.dram("l2q", [16, 64, T], F32R)
        k_h = self.dram("l2k", [16, 64, T], F32R)
        vt_h = self.dram("l2vt", [16, BPC, 16, 128, 64], F32R)
        sg_h = self.dram("l2sg", [D, T], F32R)
        ret_h = self.dram("l2ret", [D, T], F32R)
        s_ln = self.phase_ln(x_cur, xn_h, entry)
        ln_full = self.ln_full_w(s_ln)
        s_xn = self.S("a_xn"); s_pe = self.S("a_pe"); s_dv = self.S("a_dv")
        s_ev = self.S("a_ev"); s_ev2 = self.S("a_ev2")
        s_ws = self.SL("w_s", 4); s_mmp = self.S("w_mm")
        s_sts = self.SL("st_s", 8)
        bgc, w_bg = self.pc_load("L2_bg_c", 8)
        s_vt = self.SL("ln_st", 2)
        xn_t = [self.pbr[:, d * 512:(d + 1) * 512] for d in range(DB)]
        slot_mark = {}
        vt_mark = {0: (), 1: ()}
        for t in range(NT):
            for d in range(DB):
                kb.dma("scalar", xn_t[d], xn_h[d * 128:(d + 1) * 128, t * 512:(t + 1) * 512],
                       waits=ln_full + ([(s_mmp, s_mmp.n)] if t else []), sem=s_xn)
            xn_w = [(s_xn, s_xn.n)]
            for Wn, kind in (("L2_Wq_t", "q"), ("L2_Wk_t", "k"), ("L2_Wv_t", "v"), ("L2_Wg_t", "g")):
                def evac(do, ps, w, kind=kind, t=t):
                    slot = do % 8
                    tile = self.hu(4 + slot)
                    g = list(slot_mark.get(slot, ()))
                    if kind == "g":
                        kb.act(tile, ps, AF.Silu, bias=bgc[:, do:do + 1],
                               waits=w + g + w_bg, incs=[(s_ev2, 1)])
                        ev = (s_ev2, s_ev2.n)
                        kb.dma("gpsimd", sg_h[do * 128:(do + 1) * 128, t * 512:(t + 1) * 512],
                               tile, waits=[ev], sem=s_sts[slot])
                        psfree = [ev]
                    elif kind in ("q", "k"):
                        kb.copy("vector", tile, ps, waits=w + g, incs=[(s_ev, 1)])
                        ev = (s_ev, s_ev.n)
                        dst = q_h if kind == "q" else k_h
                        for hp in range(2):
                            kb.dma("gpsimd", dst[do * 2 + hp, :, t * 512:(t + 1) * 512],
                                   tile[hp * 64:hp * 64 + 64, :], waits=[ev], sem=s_sts[slot])
                        psfree = [ev]
                    else:
                        kb.copy("vector", tile, ps, waits=w + g, incs=[(s_ev, 1)])
                        ev = (s_ev, s_ev.n)
                        b = t // 4
                        for sb4 in range(4):
                            for hp in range(2):
                                p2 = (sb4 * 2 + hp) % 2
                                bkt = 4 + p2
                                sblk = (t % 4) * 4 + sb4
                                pt = psb[bkt][0:128, 0:64]
                                kb.trans(pt, tile[hp * 64:hp * 64 + 64,
                                                  sb4 * 128:(sb4 + 1) * 128].bitcast(F32),
                                         self.ident2[hp * 64:hp * 64 + 64, 0:64],
                                         waits=[ev] + self.bw(bkt) + self.init_w, incs=[(s_pe, 1)])
                                vtt = self.hu(12 + p2)[:, 0:64]
                                kb.copy("vector", vtt, pt.bitcast(F32R),
                                        waits=[(s_pe, s_pe.n)] + list(vt_mark[p2]), incs=[(s_ev, 1)])
                                self.bmark[bkt] = [(s_ev, s_ev.n)]
                                kb.dma("gpsimd", vt_h[do * 2 + hp, b, sblk], vtt,
                                       waits=[(s_ev, s_ev.n)], sem=s_vt[p2])
                                vt_mark[p2] = [(s_vt[p2], s_vt[p2].n)]
                        psfree = [ev]
                        slot_mark[slot] = [(s_pe, s_pe.n)]
                        return psfree
                    slot_mark[slot] = [(s_sts[slot], s_sts[slot].n)]
                    return psfree
                self.stream_proj(WD[Wn], DB, DB, lambda k: xn_t[k], evac,
                                 xn_w + self.init_w, s_ws, s_mmp)
        pr_done = [(s, s.n) for s in s_sts] + [(s_vt[0], s_vt[0].n), (s_vt[1], s_vt[1].n)]
        if self.dev_stop == "l2proj":
            s_cp = kb.sem()
            for h in range(16):
                kb.dma("sync", x_next[h * 64:(h + 1) * 64, :], q_h[h].bitcast(F32),
                       waits=pr_done, sem=s_cp)
            return s_cp

        s_q = [self.S("sc_rc"), self.S("sc_tile")]
        s_k = [self.S("l1_wx"), self.S("l1_wd")]
        s_v = self.SL("ld_q", 2)
        s_dm = self.S("a_sum"); s_rst = self.SL("st_c", 2); s_cm = self.S("q_wl")
        qsl = [self.pbr[0:64, i * 2048:(i + 1) * 2048] for i in range(2)]
        ksl = [self.pbr[0:64, 4096 + i * 2048:4096 + (i + 1) * 2048] for i in range(2)]
        dmk = [self.pbf[:, j * 512:(j + 1) * 512] for j in range(4)]
        rfb = self.pbf[:, 2048:2560]
        gcolc, w_gc = self.pc_load("L2_gcol_c", 16)
        GAM = [1.0 - 2.0 ** (-5.0 - hh) for hh in range(16)]
        vsl = [[self.ph[:, (16 + 2 * i) * 512 + s * 64:(16 + 2 * i) * 512 + (s + 1) * 64]
                for s in range(16)] for i in range(2)]
        smt = [self.hu(20 + i) for i in range(4)]
        rout = [self.hu(24 + i) for i in range(2)]
        qk_guard = [(), ()]
        rout_mark = [(), ()]
        for h in range(16):
            cw0 = [(s_pe, s_pe.n), (s_dm, s_dm.n)]
            kb.dma("sync", rfb, WD["c_rf"][h], waits=cw0, sem=s_cm)
            for j in range(4):
                kb.dma("sync", dmk[j], WD["c_dmask"][h, j], waits=cw0, sem=s_cm)
            cm_w = [(s_cm, s_cm.n)]
            for b in range(BPC):
                par = (h * BPC + b) % 2
                gq = pr_done + list(qk_guard[par])
                kb.dma("sync", qsl[par], q_h[h, :, b * SEQ:(b + 1) * SEQ], waits=gq, sem=s_q[par])
                kb.dma("sync", ksl[par], k_h[h, :, b * SEQ:(b + 1) * SEQ], waits=gq, sem=s_k[par])
                qk_w = [(s_q[par], s_q[par].n), (s_k[par], s_k[par].n)]
                for sblk in range(16):
                    kb.dma("scalar", vsl[par][sblk], vt_h[h, b, sblk], waits=gq, sem=s_v[par])
                vw0 = (s_v[par], s_v[par].n)
                for sblk in range(16):
                    kb.ts("vector", vsl[par][sblk], vsl[par][sblk], gcolc[:, h:h + 1], None,
                          ALU.mult, waits=[vw0] + w_gc, incs=[(s_dv, 1)])
                v_w = [(s_dv, s_dv.n)]
                for tch in range(4):
                    nsb = 4 * tch + 4
                    bkr = 6 + tch % 2
                    ps_r = psb[bkr][0:64, 0:512]
                    for sblk in range(nsb):
                        bks = sblk % 2
                        ps_s = psb[bks][:, 0:512]
                        kb.mm(ps_s, ksl[par][:, sblk * 128:(sblk + 1) * 128],
                              qsl[par][:, tch * 512:(tch + 1) * 512],
                              True, True, waits=qk_w + self.bw(bks), incs=[(s_pe, 1)])
                        pev = s_pe.n
                        r4 = sblk % 4
                        sm = smt[r4]
                        delta0 = tch * 512 - sblk * 128
                        if delta0 >= 128:
                            c0 = (GAM[h] ** delta0) / 8.0
                            kb.stt("vector", sm, ps_s, c0, rfb, ALU.mult, ALU.mult,
                                   waits=[(s_pe, pev)] + cm_w, incs=[(s_dm, 1)])
                        else:
                            jrel = sblk - 4 * tch
                            kb.stt("vector", sm, ps_s, 1.0 / 8.0, dmk[jrel], ALU.mult, ALU.mult,
                                   waits=[(s_pe, pev)] + cm_w, incs=[(s_dm, 1)])
                        self.bmark[bks] = [(s_dm, s_dm.n)]
                        smw = (s_dm, s_dm.n)
                        kb.mm(ps_r, vsl[par][sblk], sm, sblk == 0, sblk == nsb - 1,
                              waits=[smw] + v_w + (self.bw(bkr) if sblk == 0 else []),
                              incs=[(s_pe, 1)])
                    t2 = tch % 2
                    kb.copy("vector", rout[t2][0:64, :], ps_r,
                            waits=[(s_pe, s_pe.n)] + list(rout_mark[t2]), incs=[(s_dv, 1)])
                    self.bmark[bkr] = [(s_dv, s_dv.n)]
                    kb.dma("gpsimd",
                           ret_h[h * 64:(h + 1) * 64, b * SEQ + tch * 512:b * SEQ + (tch + 1) * 512],
                           rout[t2][0:64, :], waits=[(s_dv, s_dv.n)], sem=s_rst[t2])
                    rout_mark[t2] = [(s_rst[t2], s_rst[t2].n)]
                qk_guard[par] = [(s_pe, s_pe.n)]
        ret_done = [(s_rst[0], s_rst[0].n), (s_rst[1], s_rst[1].n)]
        if self.dev_stop == "l2ret":
            s_cp = kb.sem()
            kb.dma("sync", x_next[:, :], ret_h[:, :].bitcast(F32), waits=ret_done, sem=s_cp)
            return s_cp

        s_moe = self.moe_ffn(xn_h, x_cur, xm_h, ret_done + [(s_mmp, s_mmp.n)],
                             "L2_", True, False)
        if self.dev_stop == "l2moe":
            s_cp = kb.sem()
            kb.dma("sync", x_next[:, :], xm_h[:, :], waits=[(s_moe, s_moe.n)], sem=s_cp)
            return s_cp

        return self.l2_final(ret_h, sg_h, xm_h, x_next, [(s_moe, s_moe.n)])

    def l2_final(self, ret_h, sg_h, xm_h, x_next, entry):
        kb, psb, WD = self.kb, self.psb, self.WD
        s_pe = self.S("a_pe"); s_dv = self.S("a_dv")
        s_ws = self.SL("w_s", 4); s_mmp = self.S("w_mm")
        s_rl = self.SL("ln_x", 2); s_sgl = self.SL("wo_xt", 2)
        s_xm = self.SL("f_xm", 2); s_st = self.SL("wo_st", 2)
        s_sq = self.S("ln_sq"); s_row = self.S("ln_row"); s_acr = self.S("ln_acr")
        s_bc = self.S("ln_bc"); s_nm = self.S("ln_nm"); s_g = self.S("wo_g")
        nm0 = s_nm.n; bc0 = s_bc.n
        gma, w_ga = self.pc_load("L2_lng_c", 8)
        bta, w_bt = self.pc_load("L2_lnb_c", 8)
        boc, w_bo = self.pc_load("L2_bo_c", 8)
        ret_t = [[self.hu(r * 24 + d) for d in range(DB)] for r in range(2)]
        sq_t = [[self.hu(r * 24 + 8 + d) for d in range(DB)] for r in range(2)]
        ln_t = [[self.hu(r * 24 + 16 + d) for d in range(DB)] for r in range(2)]
        sgt = [self.hu(48 + i) for i in range(2)]
        mean = self.ps_[0:1, 0:512]; q = self.ps_[0:1, 512:1024]
        var = self.ps_[0:1, 1024:1536]; lnv = self.ps_[0:1, 1536:2048]
        nmean_r = self.pbr[0:1, 8448:8960]
        rstd_r = self.pbr[0:1, 8960:9472]
        xt = [self.pbf[:, i * 512:(i + 1) * 512] for i in range(2)]
        ot = [self.pbf[:, 1024 + i * 512:1024 + (i + 1) * 512] for i in range(2)]
        sg_mark = [(), ()]
        proj_done = {}
        for t in range(NT):
            r = t % 2
            guard = [(s_nm, nm0 + 32 * (t - 1))] if t >= 2 else []
            pj = list(proj_done.get(t - 2, ()))
            for d in range(DB):
                kb.dma("sync", ret_t[r][d], ret_h[d * 128:(d + 1) * 128, t * 512:(t + 1) * 512],
                       waits=entry + guard + pj, sem=s_rl[r])
            rw = [(s_rl[r], s_rl[r].n)]
            for d in range(DB):
                kb.act(sq_t[r][d], ret_t[r][d], AF.Square, waits=rw + guard, incs=[(s_sq, 1)])
            psS = psb[r][0:1, 0:512]
            psQ = psb[2 + r][0:1, 0:512]
            for d in range(DB):
                kb.mm(psS, self.onec_r, ret_t[r][d], d == 0, d == DB - 1,
                      waits=entry + self.init_w + self.bw(r) + rw,
                      incs=[(s_mmp, 1)] if d == DB - 1 else ())
            for d in range(DB):
                kb.mm(psQ, self.onec_r, sq_t[r][d], d == 0, d == DB - 1,
                      waits=self.bw(2 + r) + [(s_sq, s_sq.n)],
                      incs=[(s_mmp, 1)] if d == DB - 1 else ())
            mmv = s_mmp.n
            row_g = [(s_bc, bc0 + 2 * t)] if t >= 1 else []
            kb.ts("vector", mean, psS, 1.0 / D, None, ALU.mult,
                  waits=[(s_mmp, mmv - 1)] + row_g, incs=[(s_row, 1)])
            kb.ts("vector", q, psQ, 1.0 / D, None, ALU.mult,
                  waits=[(s_mmp, mmv)], incs=[(s_row, 1)])
            rv = s_row.n
            self.bmark[r] = [(s_row, rv)]
            self.bmark[2 + r] = [(s_row, rv)]
            kb.tt("vector", var, mean, mean, ALU.mult,
                  waits=[(s_row, rv), (s_acr, s_acr.n)], incs=[(s_row, 1)])
            kb.tt("vector", var, q, var, ALU.subtract, waits=[(s_row, s_row.n)], incs=[(s_row, 1)])
            kb.ts("vector", nmean_r, mean, -1.0, None, ALU.mult, waits=row_g, incs=[(s_row, 1)])
            kb.act(lnv, var, AF.Ln, bias=self.epsc[0:1, 1:2],
                   waits=row_g + self.init_w + [(s_row, rv + 2)], incs=[(s_acr, 1)])
            kb.act(rstd_r, lnv, AF.Exp, scale=-0.5, waits=[(s_acr, s_acr.n)], incs=[(s_acr, 1)])
            psMB = psb[4][:, 0:512]
            psRB = psb[5][:, 0:512]
            kb.mm(psMB, self.ones_r, nmean_r, True, True,
                  waits=self.bw(4) + [(s_row, s_row.n)], incs=[(s_bc, 1)])
            kb.mm(psRB, self.ones_r, rstd_r, True, True,
                  waits=self.bw(5) + [(s_acr, s_acr.n)], incs=[(s_bc, 1)])
            bcv = s_bc.n
            nmt = s_nm.n
            for d in range(DB):
                kb.tt("vector", ln_t[r][d], ret_t[r][d], psMB, ALU.add,
                      waits=[(s_bc, bcv)] + pj, incs=[(s_nm, 1)])
            for d in range(DB):
                kb.tt("vector", ln_t[r][d], ln_t[r][d], psRB, ALU.mult,
                      waits=[(s_nm, nmt + d + 1)], incs=[(s_nm, 1)])
            dv1 = s_nm.n
            for d in range(DB):
                kb.ts("vector", ln_t[r][d], ln_t[r][d], gma[:, d:d + 1], bta[:, d:d + 1],
                      ALU.mult, ALU.add,
                      waits=[(s_nm, dv1 - DB + d + 1)] + w_ga + w_bt, incs=[(s_nm, 1)])
            dv2 = s_nm.n
            self.bmark[4] = [(s_nm, dv2)]
            self.bmark[5] = [(s_nm, dv2)]
            for d in range(DB):
                i2 = d % 2
                kb.dma("scalar", sgt[i2], sg_h[d * 128:(d + 1) * 128, t * 512:(t + 1) * 512],
                       waits=entry + list(sg_mark[i2]), sem=s_sgl[i2])
                kb.tt("vector", ln_t[r][d], ln_t[r][d], sgt[i2], ALU.mult,
                      waits=[(s_sgl[i2], s_sgl[i2].n), (s_nm, dv2 - DB + d + 1)], incs=[(s_nm, 1)])
                sg_mark[i2] = [(s_nm, s_nm.n)]
            mix_w = [(s_nm, s_nm.n)]

            def evac_o(do, ps, w, t=t):
                i2 = do % 2
                kb.dma("scalar", xt[i2], xm_h[do * 128:(do + 1) * 128, t * 512:(t + 1) * 512],
                       waits=entry + [(s_st[i2], s_st[i2].n)], sem=s_xm[i2])
                xv = (s_xm[i2], s_xm[i2].n)
                kb.ts("vector", ot[i2], ps, boc[:, do:do + 1], None, ALU.add,
                      waits=w + w_bo + [(s_st[i2], s_st[i2].n)], incs=[(s_dv, 1)])
                kb.tt("gpsimd", ot[i2], ot[i2], xt[i2], ALU.add,
                      waits=[xv, (s_dv, s_dv.n)], incs=[(s_g, 1)])
                kb.dma("gpsimd", x_next[do * 128:(do + 1) * 128, t * 512:(t + 1) * 512],
                       ot[i2], waits=[(s_g, s_g.n)], sem=s_st[i2])
                return [(s_dv, s_dv.n)]
            self.stream_proj(WD["L2_Wo_t"], DB, DB, lambda k, r=r: ln_t[r][k], evac_o,
                             mix_w, s_ws, s_mmp)
            proj_done[t] = [(s_mmp, s_mmp.n)]
        s_done = kb.sem()
        kb.emit("gpsimd", lambda e: e.engine_nop(),
                waits=[(s_st[0], s_st[0].n), (s_st[1], s_st[1].n)], incs=[(s_done, 1)])
        return s_done

    def layer3(self, x_cur, x_next, entry=()):
        kb, psb, WD = self.kb, self.psb, self.WD
        entry = list(entry)
        xn_h = self.dram("xn3", [D, T], F32R)
        xm_h = self.dram("xm3", [D, T], F32)
        q_h = self.dram("l3q", [16, 64, T], F32R)
        kt_h = self.dram("l3kt", [16, BPC, 16, 128, 64], F32R)
        vt_h = self.dram("l3vt", [16, BPC, 16, 128, 64], F32R)
        lin_h = self.dram("l3lin", [D, T], F32R)
        z_h = self.dram("l3z", [32, SEQ], F32R)
        s_ln = self.phase_ln(x_cur, xn_h, entry)
        ln_full = self.ln_full_w(s_ln)
        s_xn = self.S("a_xn"); s_pe = self.S("a_pe"); s_dv = self.S("a_dv"); s_ac = self.S("a_ac")
        s_ev = self.S("a_ev"); s_ws = self.SL("w_s", 4); s_mmp = self.S("w_mm")
        s_sts = self.SL("st_s", 8)
        bqc, w_bq = self.pc_load("L3_bqkv_c", 24)
        s_vt = self.SL("ln_st", 2)
        xn_t = [self.pbr[:, d * 512:(d + 1) * 512] for d in range(DB)]
        slot_mark = {}
        vt_mark = {0: (), 1: ()}
        for t in range(NT):
            for d in range(DB):
                kb.dma("scalar", xn_t[d], xn_h[d * 128:(d + 1) * 128, t * 512:(t + 1) * 512],
                       waits=ln_full + ([(s_mmp, s_mmp.n)] if t else []), sem=s_xn)
            xn_w = [(s_xn, s_xn.n)]

            def evac_qkv(do, ps, w, t=t):
                kind = do // 8
                dd = do % 8
                tile = self.hu(4 + dd)
                g = list(slot_mark.get(dd, ()))
                if kind < 2:
                    scr = self.pbf[:, (dd % 4) * 512:(dd % 4 + 1) * 512]
                    kb.ts("vector", scr, ps, bqc[:, do:do + 1], 0.0, ALU.add, ALU.min,
                          waits=w + g + w_bq + [(s_ev, s_ev.n)], incs=[(s_dv, 1)])
                    kb.act(scr, scr, AF.Exp, waits=[(s_dv, s_dv.n)], incs=[(s_ac, 1)])
                    kb.ts("vector", tile, ps, bqc[:, do:do + 1], 0.0, ALU.add, ALU.max,
                          waits=w + g + w_bq, incs=[(s_dv, 1)])
                    psfree = [(s_dv, s_dv.n)]
                    kb.tt("vector", tile, tile, scr, ALU.add,
                          waits=[(s_ac, s_ac.n), (s_dv, s_dv.n)], incs=[(s_ev, 1)])
                else:
                    kb.ts("vector", tile, ps, bqc[:, do:do + 1], None, ALU.add,
                          waits=w + g + w_bq, incs=[(s_ev, 1)])
                    psfree = [(s_ev, s_ev.n)]
                ev = (s_ev, s_ev.n)
                if kind == 0:
                    for hp in range(2):
                        kb.dma("gpsimd", q_h[dd * 2 + hp, :, t * 512:(t + 1) * 512],
                               tile[hp * 64:hp * 64 + 64, :], waits=[ev], sem=s_sts[dd])
                else:
                    dst = kt_h if kind == 1 else vt_h
                    b = t // 4
                    for sb4 in range(4):
                        for hp in range(2):
                            p2 = (sb4 * 2 + hp) % 2
                            bkt = 4 + p2
                            sblk = (t % 4) * 4 + sb4
                            pt = psb[bkt][0:128, 0:64]
                            kb.trans(pt, tile[hp * 64:hp * 64 + 64,
                                              sb4 * 128:(sb4 + 1) * 128].bitcast(F32),
                                     self.ident2[hp * 64:hp * 64 + 64, 0:64],
                                     waits=[ev] + self.bw(bkt) + self.init_w, incs=[(s_pe, 1)])
                            vtt = self.hu(12 + p2)[:, 0:64]
                            kb.copy("vector", vtt, pt.bitcast(F32R),
                                    waits=[(s_pe, s_pe.n)] + list(vt_mark[p2]), incs=[(s_ev, 1)])
                            self.bmark[bkt] = [(s_ev, s_ev.n)]
                            kb.dma("gpsimd", dst[dd * 2 + hp, b, sblk], vtt,
                                   waits=[(s_ev, s_ev.n)], sem=s_vt[p2])
                            vt_mark[p2] = [(s_vt[p2], s_vt[p2].n)]
                    slot_mark[dd] = [(s_pe, s_pe.n)]
                    return psfree
                slot_mark[dd] = [(s_sts[dd], s_sts[dd].n)]
                return psfree
            self.stream_proj(WD["L3_wqkv_t"], 24, DB, lambda k: xn_t[k], evac_qkv,
                             xn_w + self.init_w, s_ws, s_mmp)
        pr_done = [(s, s.n) for s in s_sts] + [(s_vt[0], s_vt[0].n), (s_vt[1], s_vt[1].n)]
        if self.dev_stop == "l3proj":
            s_cp = kb.sem()
            for h in range(16):
                kb.dma("sync", x_next[h * 64:(h + 1) * 64, :], q_h[h].bitcast(F32),
                       waits=pr_done, sem=s_cp)
            return s_cp

        s_kl = self.SL("f_w2", 2); s_zst = self.S("sc_st"); s_kvd = self.S("a_qkv")
        s_ql2 = [self.S("l1_wx"), self.S("l1_wd")]
        s_zl = self.S("sc_rb"); s_lst = self.SL("st_q", 2)
        kvsb = self.pbr[0:64, 0:2048]
        kssb = self.pbr[0:64, 2048:2080]
        ktile = [self.hu(0 + i) for i in range(2)]
        vtile = [self.hu(2 + i) for i in range(2)]
        for h in range(16):
            for b in range(BPC):
                hb = h * BPC + b
                ps_kv = psb[0][0:64, 0:64]
                ps_ks = psb[1][0:64, 0:2]
                for sblk in range(16):
                    i2 = sblk % 2
                    gk = pr_done + [(s_pe, s_pe.n)]
                    kb.dma("sync", ktile[i2][:, 0:64], kt_h[h, b, sblk], waits=gk, sem=s_kl[i2])
                    kb.dma("sync", vtile[i2][:, 0:64], vt_h[h, b, sblk], waits=gk, sem=s_kl[i2])
                    kw = (s_kl[i2], s_kl[i2].n)
                    kb.mm(ps_kv, ktile[i2][:, 0:64], vtile[i2][:, 0:64], sblk == 0, sblk == 15,
                          waits=[kw] + (self.bw(0) if sblk == 0 else []), incs=[(s_pe, 1)])
                    kb.mm(ps_ks, ktile[i2][:, 0:64], self.ones2_r, sblk == 0, sblk == 15,
                          waits=[kw] + (self.bw(1) if sblk == 0 else []) + self.init_w,
                          incs=[(s_pe, 1)])
                pev = s_pe.n
                kb.copy("vector", kvsb[:, hb * 64:(hb + 1) * 64], ps_kv,
                        waits=[(s_pe, pev), (s_kvd, s_kvd.n)], incs=[(s_kvd, 1)])
                kb.copy("vector", kssb[:, hb:hb + 1], ps_ks[:, 0:1],
                        waits=[(s_pe, pev), (s_kvd, s_kvd.n)], incs=[(s_kvd, 1)])
                self.bmark[0] = [(s_kvd, s_kvd.n)]
                self.bmark[1] = [(s_kvd, s_kvd.n)]
        kv_done = [(s_kvd, s_kvd.n)]
        qtl = [self.ph[:, (4 + 4 * i) * 512:(8 + 4 * i) * 512] for i in range(2)]
        zrow = self.pbr[0:1, 4096:6144]
        for h in range(16):
            for b in range(BPC):
                hb = h * BPC + b
                i2 = hb % 2
                kb.dma("sync", qtl[i2][0:64, :], q_h[h, :, b * SEQ:(b + 1) * SEQ],
                       waits=pr_done + [(s_pe, s_pe.n)], sem=s_ql2[i2])
                qw = (s_ql2[i2], s_ql2[i2].n)
                ps_z = psb[2][0:1, 0:512]
                for tc in range(4):
                    kb.mm(ps_z, kssb[:, hb:hb + 1], qtl[i2][0:64, tc * 512:(tc + 1) * 512],
                          True, True, waits=[qw] + kv_done + self.bw(2), incs=[(s_pe, 1)])
                    kb.act(zrow[0:1, tc * 512:(tc + 1) * 512], ps_z, AF.Ln,
                           bias=self.epsc[0:1, 3:4],
                           waits=[(s_pe, s_pe.n), (s_zst, s_zst.n)] + self.init_w,
                           incs=[(s_ac, 1)])
                    self.bmark[2] = [(s_ac, s_ac.n)]
                kb.act(zrow, zrow, AF.Exp, scale=-1.0, waits=[(s_ac, s_ac.n)], incs=[(s_ac, 1)])
                kb.dma("gpsimd", z_h[hb, :], zrow, waits=[(s_ac, s_ac.n)], sem=s_zst)
        z_done = [(s_zst, s_zst.n)]
        if self.dev_stop == "l3kv":
            s_cp = kb.sem()
            kb.dma("sync", x_next[0:32, 0:2048], z_h[:, :].bitcast(F32), waits=z_done, sem=s_cp)
            kb.dma("sync", x_next[0:64, 2048:4096], kvsb.bitcast(F32), waits=z_done + kv_done,
                   sem=s_cp)
            return s_cp

        zsc = self.pbr[0:1, 6144:6656]
        zb = self.pbf[:, 0:512]
        lout = [self.hu(12 + i) for i in range(2)]
        lo_mark = [(), ()]
        for h in range(16):
            for b in range(BPC):
                hb = h * BPC + b
                i2 = hb % 2
                kb.dma("sync", qtl[i2][0:64, :], q_h[h, :, b * SEQ:(b + 1) * SEQ],
                       waits=pr_done + [(s_pe, s_pe.n)], sem=s_ql2[i2])
                qw = (s_ql2[i2], s_ql2[i2].n)
                for tc in range(4):
                    ps_o = psb[3][0:64, 0:512]
                    ps_zb = psb[4][:, 0:512]
                    kb.mm(ps_o, kvsb[:, hb * 64:(hb + 1) * 64],
                          qtl[i2][0:64, tc * 512:(tc + 1) * 512],
                          True, True, waits=[qw] + kv_done + self.bw(3), incs=[(s_pe, 1)])
                    kb.dma("sync", zsc, z_h[hb:hb + 1, tc * 512:(tc + 1) * 512],
                           waits=z_done + [(s_pe, s_pe.n)], sem=s_zl)
                    kb.mm(ps_zb, self.ones_r, zsc, True, True,
                          waits=[(s_zl, s_zl.n)] + self.bw(4) + self.init_w, incs=[(s_pe, 1)])
                    kb.copy("vector", zb, ps_zb, waits=[(s_pe, s_pe.n), (s_dv, s_dv.n)],
                            incs=[(s_dv, 1)])
                    self.bmark[4] = [(s_dv, s_dv.n)]
                    kb.tt("vector", lout[i2][0:64, :], ps_o, zb[0:64, :], ALU.mult,
                          waits=[(s_pe, s_pe.n), (s_dv, s_dv.n)] + list(lo_mark[i2]),
                          incs=[(s_dv, 1)])
                    self.bmark[3] = [(s_dv, s_dv.n)]
                    kb.dma("gpsimd",
                           lin_h[h * 64:(h + 1) * 64, b * SEQ + tc * 512:b * SEQ + (tc + 1) * 512],
                           lout[i2][0:64, :], waits=[(s_dv, s_dv.n)], sem=s_lst[i2])
                    lo_mark[i2] = [(s_lst[i2], s_lst[i2].n)]
        lin_done = [(s_lst[0], s_lst[0].n), (s_lst[1], s_lst[1].n)]
        if self.dev_stop == "l3lin":
            s_cp = kb.sem()
            kb.dma("sync", x_next[:, :], lin_h[:, :].bitcast(F32), waits=lin_done, sem=s_cp)
            return s_cp

        s_moe = self.moe_ffn(xn_h, x_cur, xm_h, lin_done + [(s_mmp, s_mmp.n)],
                             "L3_", False, True)
        if self.dev_stop == "l3moe":
            s_cp = kb.sem()
            kb.dma("sync", x_next[:, :], xm_h[:, :], waits=[(s_moe, s_moe.n)], sem=s_cp)
            return s_cp

        return self.final_proj(lin_h, xm_h, x_next, [(s_moe, s_moe.n)],
                               "L3_wproj_t", "L3_bproj_c")

    def final_proj(self, lin_h, xm_h, x_next, entry, wname, bname):
        kb, psb, WD = self.kb, self.psb, self.WD
        s_dv = self.S("a_dv")
        s_ws = self.SL("w_s", 4); s_mmp = self.S("w_mm")
        s_ll = self.SL("ln_x", 2); s_xm = self.SL("f_xm", 2); s_st = self.SL("wo_st", 2)
        s_g = self.S("wo_g")
        bpc_, w_bp = self.pc_load(bname, 8)
        lt = [[self.hu(r * 8 + d) for d in range(DB)] for r in range(2)]
        xt = [self.pbf[:, i * 512:(i + 1) * 512] for i in range(2)]
        ot = [self.pbf[:, 1024 + i * 512:1024 + (i + 1) * 512] for i in range(2)]
        proj_done = {}
        for t in range(NT):
            r = t % 2
            pj = list(proj_done.get(t - 2, ()))
            for d in range(DB):
                kb.dma("sync", lt[r][d], lin_h[d * 128:(d + 1) * 128, t * 512:(t + 1) * 512],
                       waits=entry + pj, sem=s_ll[r])
            lw = [(s_ll[r], s_ll[r].n)]

            def evac_p(do, ps, w, t=t):
                i2 = do % 2
                kb.dma("scalar", xt[i2], xm_h[do * 128:(do + 1) * 128, t * 512:(t + 1) * 512],
                       waits=entry + [(s_st[i2], s_st[i2].n)], sem=s_xm[i2])
                xv = (s_xm[i2], s_xm[i2].n)
                kb.ts("vector", ot[i2], ps, bpc_[:, do:do + 1], None, ALU.add,
                      waits=w + w_bp + [(s_st[i2], s_st[i2].n)], incs=[(s_dv, 1)])
                kb.tt("gpsimd", ot[i2], ot[i2], xt[i2], ALU.add,
                      waits=[xv, (s_dv, s_dv.n)], incs=[(s_g, 1)])
                kb.dma("gpsimd", x_next[do * 128:(do + 1) * 128, t * 512:(t + 1) * 512],
                       ot[i2], waits=[(s_g, s_g.n)], sem=s_st[i2])
                return [(s_dv, s_dv.n)]
            self.stream_proj(WD[wname], DB, DB, lambda k, r=r: lt[r][k], evac_p,
                             lw, s_ws, s_mmp)
            proj_done[t] = [(s_mmp, s_mmp.n)]
        s_done = kb.sem()
        kb.emit("gpsimd", lambda e: e.engine_nop(),
                waits=[(s_st[0], s_st[0].n), (s_st[1], s_st[1].n)], incs=[(s_done, 1)])
        return s_done

    def build(self):
        self.init_consts()
        self.pc_preload()
        x_cur = self.x_in
        last = None
        for li in self.layers:
            x_next = self.y_out if li == self.layers[-1] else self.dram(f"xr{li}", [D, T])
            ent = [] if last is None else [(last, last.n)]
            if li == 0:
                last = self.layer0(x_cur, x_next)
            elif li == 1:
                last = self.layer1(x_cur, x_next, ent)
            elif li == 2:
                last = self.layer2(x_cur, x_next, ent)
            elif li == 3:
                last = self.layer3(x_cur, x_next, ent)
            else:
                raise NotImplementedError(li)
            x_cur = x_next
        self.kb.wait_only("sync", [(last, last.n)])
        self.kb.finalize()
        self.stack.close()
        return self.nc


def prep_weights(inputs, layers=(0, 1, 2, 3)):
    w = {}
    w["c_ones_row"] = np.ones((1, 128), np.float32)
    w["c_ones_col"] = np.ones((128, 1), np.float32)
    w["c_ident"] = np.eye(128, dtype=np.float32)
    w["c_ident2"] = np.vstack([np.eye(64, dtype=np.float32)] * 2)
    w["c_zeros"] = np.zeros((128, 4), np.float32)
    w["c_eps"] = np.broadcast_to(np.array([1e-5, 1e-3, 1e-12, 1e-8, 1.0, 0.0, 0.5, 2.0], np.float32), (128, 8)).copy()
    w["c_ones2"] = np.ones((128, 2), np.float32)
    if 0 in layers:
        for nm, src in (("L0_Wq_l_t", "L0_attn_Wq_l"), ("L0_Wk_l_t", "L0_attn_Wk_l"),
                        ("L0_Wv_l_t", "L0_attn_Wv_l"), ("L0_Wo_l_t", "L0_attn_Wo_l"),
                        ("L0_Wq_g_t", "L0_attn_Wq_g"), ("L0_Wk_g_t", "L0_attn_Wk_g"),
                        ("L0_Wv_g_t", "L0_attn_Wv_g"), ("L0_Wo_g_t", "L0_attn_Wo_g")):
            w[nm] = tile_lhsT(np.asarray(inputs[src]))
        w["L0_W1_t"] = tile_lhsT(np.asarray(inputs["L0_ffn_W1"]))
        w["L0_W2_t"] = tile_lhsT(np.asarray(inputs["L0_ffn_W2"]))
        w["L0_b1_c"] = col_stage(np.asarray(inputs["L0_ffn_b1"]))
        w["L0_b2_c"] = col_stage(np.asarray(inputs["L0_ffn_b2"]))
    if 1 in layers:
        w["L1_W_in_t"] = tile_lhsT(np.asarray(inputs["L1_attn_W_in"]))
        cw = np.asarray(inputs["L1_attn_conv_w"]).reshape(16, 128, 4)
        w["L1_conv_w_c"] = np.ascontiguousarray(cw.transpose(1, 0, 2).reshape(128, 64))
        w["L1_conv_b_c"] = col_stage(np.asarray(inputs["L1_attn_conv_b"]))
        wx = np.asarray(inputs["L1_attn_W_x"]).reshape(16, 128, 96)
        w["L1_Wx_s"] = np.ascontiguousarray(wx.transpose(1, 0, 2).reshape(128, 1536))
        w["L1_Wdt_s"] = np.ascontiguousarray(np.asarray(inputs["L1_attn_W_dt"]))
        w["L1_b_dt_c"] = col_stage(np.asarray(inputs["L1_attn_b_dt"]))
        al = np.asarray(inputs["L1_attn_A_log"]).reshape(16, 128, 16)
        w["L1_Alog_c"] = np.ascontiguousarray(al.transpose(1, 0, 2).reshape(128, 256))
        w["L1_D_c"] = col_stage(np.asarray(inputs["L1_attn_D"]))
        w["L1_Wout_t"] = tile_lhsT(np.asarray(inputs["L1_attn_W_out"]))
        w["L1_bout_c"] = col_stage(np.asarray(inputs["L1_attn_b_out"]))
        qw = np.asarray(inputs["L1_ffn_qweights"])
        w["L1_qT_s"] = np.ascontiguousarray(qw.transpose(0, 2, 1).reshape(6, 8, 128, 1024))
        sc = np.asarray(inputs["L1_ffn_scales"])
        srep = np.repeat(sc, 16, axis=2)
        w["L1_srepT_s"] = np.ascontiguousarray(srep.transpose(0, 2, 1).reshape(6, 8, 128, 1024))
        la = np.asarray(inputs["L1_ffn_lora_a"])
        w["L1_la_s"] = np.ascontiguousarray(
            la.transpose(0, 2, 1).reshape(6, 8, 128, 32).transpose(0, 2, 1, 3).reshape(6, 128, 256))
        lb = np.asarray(inputs["L1_ffn_lora_b"])
        w["L1_lbT_s"] = np.ascontiguousarray(lb.transpose(0, 2, 1))
        bs = np.asarray(inputs["L1_ffn_biases"])
        w["L1_b_c"] = np.ascontiguousarray(
            np.stack([col_stage(bs[i]) for i in range(6)], 0).transpose(1, 0, 2).reshape(128, 48))
        lg = np.asarray(inputs["L1_ffn_ln_gamma"])
        w["L1_lng_c"] = np.ascontiguousarray(
            np.stack([col_stage(lg[i]) for i in range(3)], 0).transpose(1, 0, 2).reshape(128, 24))
        lbta = np.asarray(inputs["L1_ffn_ln_beta"])
        w["L1_lnb_c"] = np.ascontiguousarray(
            np.stack([col_stage(lbta[i]) for i in range(3)], 0).transpose(1, 0, 2).reshape(128, 24))

    def moe_stage(pfx, sw, sb, w1, b1, w2, b2):
        w[pfx + "sw_s"] = np.ascontiguousarray(
            np.asarray(sw).reshape(8, 128, 8).transpose(1, 0, 2).reshape(128, 64))
        w[pfx + "swb_rep"] = np.ascontiguousarray(
            np.broadcast_to(np.asarray(sb), (128, 8)))
        w[pfx + "b1_c"] = np.ascontiguousarray(
            np.stack([col_stage(np.asarray(b1)[e]) for e in range(8)], 0)
            .transpose(1, 0, 2).reshape(128, 256))
        w[pfx + "b2_lhs"] = np.ascontiguousarray(np.asarray(b2))
        w[pfx + "w1_t"] = np.stack([tile_lhsT(np.asarray(w1)[e]) for e in range(8)], 0)
        w[pfx + "w2_t"] = np.stack([tile_lhsT(np.asarray(w2)[e]) for e in range(8)], 0)

    if 2 in layers:
        for nm, src in (("L2_Wq_t", "L2_attn_Wq"), ("L2_Wk_t", "L2_attn_Wk"),
                        ("L2_Wv_t", "L2_attn_Wv"), ("L2_Wg_t", "L2_attn_Wg"),
                        ("L2_Wo_t", "L2_attn_Wo")):
            w[nm] = tile_lhsT(np.asarray(inputs[src]))
        w["L2_bg_c"] = col_stage(np.asarray(inputs["L2_attn_bg"]))
        w["L2_bo_c"] = col_stage(np.asarray(inputs["L2_attn_bo"]))
        w["L2_lng_c"] = col_stage(np.asarray(inputs["L2_attn_ln_gamma"]))
        w["L2_lnb_c"] = col_stage(np.asarray(inputs["L2_attn_ln_beta"]))
        gam = 1.0 - 2.0 ** (-5.0 - np.arange(16, dtype=np.float64))
        i_idx = np.arange(512, dtype=np.float64)
        j_idx = np.arange(128, dtype=np.float64)
        w["c_rf"] = np.ascontiguousarray(np.broadcast_to(
            (gam[:, None] ** i_idx[None, :])[:, None, :], (16, 128, 512)).astype(np.float32))
        dm = np.zeros((16, 4, 128, 512), np.float64)
        for jrel in range(4):
            expo = i_idx[None, None, :] - jrel * 128
            caus = (i_idx[None, None, :] - jrel * 128 - j_idx[None, :, None]) >= 0
            dm[:, jrel] = (gam[:, None, None] ** expo) * caus
        w["c_dmask"] = dm.astype(np.float32)
        w["L2_gcol_c"] = np.ascontiguousarray(
            (gam[None, :] ** (-j_idx[:, None])).astype(np.float32))
        moe_stage("L2_", inputs["L2_ffn_switch_w"], inputs["L2_ffn_switch_b"],
                  inputs["L2_ffn_w1"], inputs["L2_ffn_b1"],
                  inputs["L2_ffn_w2"], inputs["L2_ffn_b2"])
    if 3 in layers:
        w["L3_wqkv_t"] = tile_lhsT(np.asarray(inputs["L3_attn_w_qkv"]))
        w["L3_bqkv_c"] = col_stage(np.asarray(inputs["L3_attn_b_qkv"]))
        w["L3_wproj_t"] = tile_lhsT(np.asarray(inputs["L3_attn_w_proj"]))
        w["L3_bproj_c"] = col_stage(np.asarray(inputs["L3_attn_b_proj"]))
        moe_stage("L3_", inputs["L3_ffn_wr"], inputs["L3_ffn_br"],
                  inputs["L3_ffn_w1"], inputs["L3_ffn_b1"],
                  inputs["L3_ffn_w2"], inputs["L3_ffn_b2"])
    return w


def run_layers(inputs, layers, cores=None, dev_stop="full"):
    x = np.asarray(inputs["L0_attn_x"])
    w = prep_weights(inputs, layers)
    meta = {k: (v.shape, F32 if v.dtype == np.float32 else I32) for k, v in w.items()}
    b = Builder(layers, dev_stop=dev_stop)
    b.decl_weights(meta)
    nc = b.build()
    cores = cores or list(range(NCORE))
    in_maps = []
    for c in cores:
        xc = np.ascontiguousarray(x[c * BPC:(c + 1) * BPC].transpose(2, 0, 1).reshape(D, T))
        m = {"x": xc}
        m.update(w)
        in_maps.append(m)
    res = run_bass_kernel_spmd(nc, in_maps, core_ids=cores)
    outs = []
    for i, c in enumerate(cores):
        yc = res.results[i]["y"]
        outs.append(yc.reshape(D, BPC, SEQ).transpose(1, 2, 0))
    return np.concatenate(outs, axis=0).astype(np.float32)


def kernel(**inputs):
    return run_layers(inputs, [0, 1, 2, 3])
